# version 04 (final)
# baseline (speedup 1.0000x reference)
import numpy as np
import ml_dtypes
from contextlib import ExitStack

import concourse.bass as bass
import concourse.mybir as mybir
from concourse.bass_utils import run_bass_kernel_spmd

F32 = mybir.dt.float32
BF16 = mybir.dt.bfloat16
U32 = mybir.dt.uint32
AF = mybir.ActivationFunctionType
ALU = mybir.AluOpType

NC = 8
D = 1024
B = 4
S = 8192
TOK = B * S
TSH = TOK // NC
DH = 4096
NHEAD = 4
DK = 256
CHUNK = 32
GH = 512
NEXP = 16
EF = 1024

bf16 = ml_dtypes.bfloat16

KONST = np.ascontiguousarray(np.broadcast_to(
    np.array([1e-5, 1e-6, 1e-5, 1e-8, 1.0, 0.0, 0.0, 0.0], np.float32), (128, 8)))


class Sem:
    def __init__(self, ctx, nc, name):
        self.h = ctx.enter_context(nc.semaphore(name))
        self.n = 0

    def bump(self, k=1):
        self.n += k
        return self.n


class Prog:
    ENGINES = ("sync", "scalar", "vector", "tensor", "gpsimd")

    def __init__(self, nc, ctx):
        self.nc = nc
        self.ctx = ctx
        self.ops = {e: [] for e in self.ENGINES}
        self._sems = {}

    def sem(self, name):
        if name not in self._sems:
            self._sems[name] = Sem(self.ctx, self.nc, name)
        return self._sems[name]

    def op(self, eng, fn, waits=(), inc=None, inc_by=1):
        tgt = inc.bump(inc_by) if inc is not None else None
        ih = inc.h if inc is not None else None
        wl = tuple((s.h, v) for s, v in waits if v > 0)

        def run(q, wl=wl, fn=fn, ih=ih, ib=inc_by):
            for h, v in wl:
                q.wait_ge(h, v)
            ins = fn(q)
            if ih is not None:
                ins.then_inc(ih, ib)
        self.ops[eng].append(run)
        return tgt

    def wait(self, eng, waits):
        wl = tuple((s.h, v) for s, v in waits if v > 0)

        def run(q, wl=wl):
            for h, v in wl:
                q.wait_ge(h, v)
        self.ops[eng].append(run)

    def dma(self, eng, out, in_, sem, waits=(), **kw):
        return self.op(eng, lambda q: q.dma_start(out=out, in_=in_, **kw),
                       waits=waits, inc=sem, inc_by=16)

    def finish(self):
        block = self.ctx.enter_context(self.nc.Block())
        for e in self.ENGINES:
            fns = self.ops[e]
            if fns:
                getattr(block, e)(lambda q, fs=tuple(fns): [f(q) for f in fs])


def sb(ctx, nc, name, shape, dt):
    return ctx.enter_context(nc.sbuf_tensor(name, list(shape), dt))


def psum_bank(ctx, nc, name, dt=F32):
    n = 512 if dt == F32 else 1024
    return ctx.enter_context(nc.psum_tensor(name, [128, n], dt))


def rearr_kt(ap, p=128):
    return ap.rearrange("(a p) c -> p a c", p=p)


def kt_dst(sb2d, a):
    return sb2d.rearrange("p (a c) -> p a c", a=a)


def emit_ln_phase(pr, ctx, nc, x_ap, xnt_sb, dram_out, idb, n_tiles, pref,
                  residual=None, x1_out=None, kn=None):
    s_ld = pr.sem(pref + "ld")
    s_v = pr.sem(pref + "v")
    s_a = pr.sem(pref + "a")
    s_t = pr.sem(pref + "t")
    s_e = pr.sem(pref + "e")
    s_st = pr.sem(pref + "st")

    xt = [sb(ctx, nc, pref + f"xt{i}", (128, D), F32) for i in range(2)]
    stat = sb(ctx, nc, pref + "stat", (128, 12), F32)
    mv = sb(ctx, nc, pref + "mv", (128, 2), F32)
    rstd = sb(ctx, nc, pref + "rstd", (128, 2), F32)
    xnb = sb(ctx, nc, pref + "xnb", (128, D), BF16)
    ptr = [psum_bank(ctx, nc, pref + f"ptr{i}", BF16) for i in range(2)]

    ev_tgt = [0, 0]
    xt_free = [0, 0]
    prev_vn = 0
    prev_aggr = 0
    last_tr = 0
    for i in range(n_tiles):
        bi = i % 2
        w = [(s_v, xt_free[bi])]
        lt = pr.dma("sync", xt[bi][:], x_ap[i * 128:(i + 1) * 128, :],
                    s_ld, waits=w)
        waits = [(s_ld, lt)]
        if residual is not None:
            waits = residual(i, xt[bi], waits)
        pr.op("vector", lambda q, t=xt[bi]: q.bn_stats(stat[:, 0:6], t[:, 0:512]),
              waits=waits + [(s_v, prev_aggr)])
        vb = pr.op("vector", lambda q, t=xt[bi]: q.bn_stats(stat[:, 6:12], t[:, 512:1024]),
                   inc=s_v)
        prev_aggr = pr.op("vector", lambda q: q.bn_aggr(mv[:], stat[:]),
                          waits=[(s_v, vb), (s_v, prev_vn), (s_a, s_a.n)], inc=s_v)
        aa = pr.op("scalar", lambda q: q.activation(out=rstd[:, 0:1], in_=mv[:, 1:2],
                                                    func=AF.Ln, bias=kn[:, 0:1]),
                   waits=[(s_v, prev_aggr), (s_v, prev_vn)], inc=s_a)
        ab = pr.op("scalar", lambda q: q.activation(out=rstd[:, 1:2], in_=rstd[:, 0:1],
                                                    func=AF.Exp, scale=-0.5),
                   waits=[(s_a, aa)], inc=s_a)
        vn = pr.op("vector", lambda q, t=xt[bi]: q.tensor_scalar(
            out=xnb[:], in0=t[:], scalar1=mv[:, 0:1], scalar2=rstd[:, 1:2],
            op0=ALU.subtract, op1=ALU.mult),
            waits=[(s_a, ab), (s_t, last_tr)], inc=s_v)
        prev_vn = vn
        xt_free[bi] = vn
        if x1_out is not None:
            pr.dma("sync", x1_out[i * 128:(i + 1) * 128, :], xt[bi][:], s_st,
                   waits=[(s_v, vn)])
        for j in range(8):
            pb = (i * 8 + j) % 2
            last_tr = pr.op("tensor", lambda q, j=j, pb=pb: q.transpose(
                ptr[pb][:, 0:128], xnb[:, j * 128:(j + 1) * 128], idb[:]),
                waits=[(s_v, vn), (s_e, ev_tgt[pb])], inc=s_t)
            ev_tgt[pb] = pr.op("vector", lambda q, j=j, pb=pb, i=i: q.tensor_copy(
                xnt_sb[j][:, i * 128:(i + 1) * 128], ptr[pb][:, 0:128]),
                waits=[(s_t, last_tr)], inc=s_e)
    for j in range(8):
        pr.dma("sync", dram_out[j * 128:(j + 1) * 128, :], xnt_sb[j][:], s_st,
               waits=[(s_e, s_e.n)])
    return s_st


def emit_ffn(pr, ctx, nc, big, wdr, sc_sb, hA_d, hB_d, hC_d, g_tm_d, idf, ntok):
    s_wl = pr.sem("fwl")
    s_wc = pr.sem("fwc")
    s_mm = pr.sem("fmm")
    s_ev = pr.sem("fev")
    s_ac = pr.sem("fac")
    s_hl = pr.sem("fhl")
    s_st = pr.sem("fst")

    nch = ntok // 512
    wf = [sb(ctx, nc, f"fwf{i}", (128, DH), F32) for i in range(2)]
    wb = [sb(ctx, nc, f"fwb{i}", (128, DH), BF16) for i in range(2)]
    wb2 = [sb(ctx, nc, f"fwc{i}", (128, DH), BF16) for i in range(2)]
    fsil = [sb(ctx, nc, f"fsil{i}", (128, 512), F32) for i in range(2)]
    hout = sb(ctx, nc, "fhout", (128, ntok), BF16)
    gout = sb(ctx, nc, "fgout", (128, ntok // 4), F32)
    pA = [psum_bank(ctx, nc, f"fpA{i}") for i in range(2)]
    pB = [psum_bank(ctx, nc, f"fpB{i}") for i in range(2)]

    def xnt(k):
        return big[:, k * ntok:(k + 1) * ntok]

    acA = [0, 0]
    evB = [0, 0]
    c1 = [0, 0]
    c2 = [0, 0]
    mm_w = [0, 0]
    hst = 0
    hst_all = 0
    for m in range(32):
        bi = m % 2
        l1 = pr.dma("sync", kt_dst(wf[bi][:, 0:D], 8), rearr_kt(wdr["w1t"][:, m * 128:(m + 1) * 128]),
                    s_wl, waits=[(s_wc, c2[bi])])
        c1[bi] = pr.op("vector", lambda q, bi=bi: q.tensor_copy(wb[bi][:, 0:D], wf[bi][:, 0:D]),
                       waits=[(s_wl, l1), (s_mm, mm_w[bi])], inc=s_wc)
        l2 = pr.dma("sync", kt_dst(wf[bi][:, 0:D], 8), rearr_kt(wdr["w2t"][:, m * 128:(m + 1) * 128]),
                    s_wl, waits=[(s_wc, c1[bi])])
        c2[bi] = pr.op("vector", lambda q, bi=bi: q.tensor_copy(wb2[bi][:, 0:D], wf[bi][:, 0:D]),
                       waits=[(s_wl, l2), (s_mm, mm_w[bi])], inc=s_wc)
        if m > 0:
            hst = pr.dma("sync", hA_d[(m - 1) * 128:m * 128, :], hout[:], s_st,
                         waits=[(s_ev, hst_all)])
        for c in range(nch):
            pi = c % 2
            mmA = None
            for k in range(8):
                w = [(s_wc, c2[bi]), (s_ac, acA[pi])] if k == 0 else []
                mmA = pr.op("tensor", lambda q, k=k, c=c, pi=pi, bi=bi: q.matmul(
                    pA[pi][:], wb[bi][:, k * 128:(k + 1) * 128],
                    xnt(k)[:, c * 512:(c + 1) * 512],
                    start=(k == 0), stop=(k == 7)),
                    waits=w, inc=s_mm if k == 7 else None)
            mmB = None
            for k in range(8):
                w = [(s_ev, evB[pi])] if k == 0 else []
                mmB = pr.op("tensor", lambda q, k=k, c=c, pi=pi, bi=bi: q.matmul(
                    pB[pi][:], wb2[bi][:, k * 128:(k + 1) * 128],
                    xnt(k)[:, c * 512:(c + 1) * 512],
                    start=(k == 0), stop=(k == 7)),
                    waits=w, inc=s_mm if k == 7 else None)
            acA[pi] = pr.op("scalar", lambda q, pi=pi, m=m: q.activation(
                out=fsil[pi][:], in_=pA[pi][:], func=AF.Silu,
                scale=sc_sb[:, m:m + 1]),
                waits=[(s_mm, mmA), (s_ev, evB[pi])], inc=s_ac)
            evB[pi] = pr.op("vector", lambda q, pi=pi, m=m, c=c: q.scalar_tensor_tensor(
                out=hout[:, c * 512:(c + 1) * 512], in0=pB[pi][:],
                scalar=sc_sb[:, 32 + m:32 + m + 1], in1=fsil[pi][:],
                op0=ALU.mult, op1=ALU.mult),
                waits=[(s_ac, acA[pi]), (s_mm, mmB), (s_st, hst)], inc=s_ev)
        mm_w[bi] = s_mm.n
        hst_all = s_ev.n
    hst = pr.dma("sync", hA_d[31 * 128:32 * 128, :], hout[:], s_st,
                 waits=[(s_ev, hst_all)])

    def mid_stage(src_d, dst_d, wname, sc_off, out_m, func, out_sb):
        nonlocal hst
        ac = {0: 0, 1: 0}
        cw = [0, 0]
        mw = [0, 0]
        st_t = 0
        nq = 4
        qtok = ntok // nq
        nck = qtok // 512
        KT = 32
        for tq in range(nq):
            hload = pr.dma("sync", kt_dst(big[:, 0:KT * qtok], KT),
                           rearr_kt(src_d[:, tq * qtok:(tq + 1) * qtok]),
                           s_hl, waits=[(s_mm, s_mm.n), (s_st, s_st.n)])
            for m in range(out_m):
                bi = m % 2
                l = pr.dma("sync", kt_dst(wf[bi][:, 0:DH], 32),
                           rearr_kt(wdr[wname][:, m * 128:(m + 1) * 128]),
                           s_wl, waits=[(s_wc, cw[bi])])
                cw[bi] = pr.op("vector", lambda q, bi=bi: q.tensor_copy(
                    wb[bi][:, 0:DH], wf[bi][:, 0:DH]),
                    waits=[(s_wl, l), (s_mm, mw[bi])], inc=s_wc)
                for c in range(nck):
                    pi = c % 2
                    mm = None
                    for k in range(KT):
                        w = [(s_wc, cw[bi]), (s_hl, hload), (s_ac, ac[pi])] if k == 0 else []
                        mm = pr.op("tensor", lambda q, k=k, c=c, pi=pi, bi=bi: q.matmul(
                            pA[pi][:], wb[bi][:, k * 128:(k + 1) * 128],
                            big[:, k * qtok + c * 512:k * qtok + (c + 1) * 512],
                            start=(k == 0), stop=(k == KT - 1)),
                            waits=w, inc=s_mm if k == KT - 1 else None)
                    ac[pi] = pr.op("scalar", lambda q, pi=pi, m=m, c=c: q.activation(
                        out=out_sb[:, c * 512:(c + 1) * 512], in_=pA[pi][:], func=func,
                        scale=sc_sb[:, sc_off + m:sc_off + m + 1]),
                        waits=[(s_mm, mm), (s_st, st_t)], inc=s_ac)
                mw[bi] = s_mm.n
                st_t = pr.dma("sync", dst_d[m * 128:(m + 1) * 128, tq * qtok:(tq + 1) * qtok],
                              out_sb[:, 0:qtok], s_st,
                              waits=[(s_ac, s_ac.n)])

    mid_stage(hA_d, hB_d, "h1t", 64, 32, AF.Gelu, hout)
    mid_stage(hB_d, hC_d, "h2t", 96, 32, AF.Gelu, hout)
    s_t4 = pr.sem("ft4")
    s_e4 = pr.sem("fe4")
    gtr = sb(ctx, nc, "fgtr", (128, 128), F32)
    pT = psum_bank(ctx, nc, "fpT")
    ac = {0: 0, 1: 0}
    cw = [0, 0]
    mw = [0, 0]
    ev4 = 0
    st4 = 0
    qtok = ntok // 4
    for tq in range(4):
        hload = pr.dma("sync", kt_dst(big[:, 0:32 * qtok], 32),
                       rearr_kt(hC_d[:, tq * qtok:(tq + 1) * qtok]),
                       s_hl, waits=[(s_mm, s_mm.n), (s_st, s_st.n)])
        for m in range(8):
            bi = m % 2
            l = pr.dma("sync", kt_dst(wf[bi][:, 0:DH], 32),
                       rearr_kt(wdr["w3t"][:, m * 128:(m + 1) * 128]),
                       s_wl, waits=[(s_wc, cw[bi])])
            cw[bi] = pr.op("vector", lambda q, bi=bi: q.tensor_copy(
                wb[bi][:, 0:DH], wf[bi][:, 0:DH]),
                waits=[(s_wl, l), (s_mm, mw[bi])], inc=s_wc)
            for c in range(qtok // 512):
                pi = c % 2
                mm = None
                for k in range(32):
                    w = [(s_wc, cw[bi]), (s_hl, hload), (s_ac, ac[pi])] if k == 0 else []
                    mm = pr.op("tensor", lambda q, k=k, c=c, pi=pi, bi=bi: q.matmul(
                        pA[pi][:], wb[bi][:, k * 128:(k + 1) * 128],
                        big[:, k * qtok + c * 512:k * qtok + (c + 1) * 512],
                        start=(k == 0), stop=(k == 31)),
                        waits=w, inc=s_mm if k == 31 else None)
                ac[pi] = pr.op("scalar", lambda q, pi=pi, m=m, c=c: q.activation(
                    out=gout[:, c * 512:(c + 1) * 512], in_=pA[pi][:], func=AF.Silu,
                    scale=sc_sb[:, 128 + m:128 + m + 1]),
                    waits=[(s_mm, mm), (s_t4, s_t4.n)], inc=s_ac)
            mw[bi] = s_mm.n
            for j in range(8):
                tt = pr.op("tensor", lambda q, j=j: q.transpose(
                    pT[:, 0:128], gout[:, j * 128:(j + 1) * 128], idf[:]),
                    waits=[(s_ac, s_ac.n), (s_e4, ev4)], inc=s_t4)
                ev4 = pr.op("vector", lambda q, j=j: q.tensor_copy(gtr[:], pT[:, 0:128]),
                            waits=[(s_t4, tt), (s_st, st4)], inc=s_e4)
                st4 = pr.dma("scalar", g_tm_d[(tq * 8 + j) * 128:(tq * 8 + j + 1) * 128,
                                              m * 128:(m + 1) * 128],
                             gtr[:], s_st, waits=[(s_e4, ev4)])
    return s_st


def build_A():
    nc = bass.Bass("TRN2", target_bir_lowering=False, debug=False)
    x = nc.dram_tensor("x", [TSH, D], F32, kind="ExternalInput").ap()
    idb_d = nc.dram_tensor("idb", [128, 128], BF16, kind="ExternalInput").ap()
    wdr = {}
    for nm, shp in [("w1t", (D, DH)), ("w2t", (D, DH)), ("h1t", (DH, DH)),
                    ("h2t", (DH, DH)), ("w3t", (DH, D))]:
        wdr[nm] = nc.dram_tensor(nm, list(shp), F32, kind="ExternalInput").ap()
    scv = nc.dram_tensor("scv", [128, 136], F32, kind="ExternalInput").ap()
    kn_d = nc.dram_tensor("kn", [128, 8], F32, kind="ExternalInput").ap()
    idf_d = nc.dram_tensor("idf", [128, 128], F32, kind="ExternalInput").ap()
    xnt_d = nc.dram_tensor("xnt", [D, TSH], BF16, kind="ExternalOutput").ap()
    g_tm_d = nc.dram_tensor("g_tm", [TSH, D], F32, kind="ExternalOutput").ap()
    hA_d = nc.dram_tensor("hA_s", [DH, TSH], BF16).ap()
    hB_d = nc.dram_tensor("hB_s", [DH, TSH], BF16).ap()
    hC_d = nc.dram_tensor("hC_s", [DH, TSH], BF16).ap()

    with ExitStack() as ctx:
        pr = Prog(nc, ctx)
        idb = sb(ctx, nc, "idbs", (128, 128), BF16)
        idf = sb(ctx, nc, "idfs", (128, 128), F32)
        kn = sb(ctx, nc, "kns", (128, 8), F32)
        sc_sb = sb(ctx, nc, "scs", (128, 136), F32)
        big = sb(ctx, nc, "big", (128, 8 * TSH), BF16)
        xnt_sb = [big[:, j * TSH:(j + 1) * TSH] for j in range(8)]
        s0 = pr.sem("init")
        pr.dma("sync", idb[:], idb_d, s0)
        pr.dma("sync", idf[:], idf_d, s0)
        pr.dma("sync", sc_sb[:], scv, s0)
        pr.dma("sync", kn[:], kn_d, s0)
        for e in ("sync", "vector", "scalar", "tensor"):
            pr.wait(e, [(s0, s0.n)])
        ln_st = emit_ln_phase(pr, ctx, nc, x, xnt_sb, xnt_d, idb, TSH // 128, "ln",
                              kn=kn)
        pr.wait("sync", [(ln_st, ln_st.n)])
        fst = emit_ffn(pr, ctx, nc, big, wdr, sc_sb, hA_d, hB_d, hC_d,
                       g_tm_d, idf, TSH)
        pr.wait("gpsimd", [(ln_st, ln_st.n), (fst, fst.n)])
        pr.finish()
    return nc


def _ffn_scales(inputs):
    v = np.concatenate([
        inputs["L0_ffn_w1_s"], inputs["L0_ffn_w2_s"], inputs["L0_ffn_h1_s"],
        inputs["L0_ffn_h2_s"], inputs["L0_ffn_w3_s"],
    ]).astype(np.float32)
    return np.ascontiguousarray(v.reshape(136, 128).T)


def host_prep_A(inputs):
    x2d = np.ascontiguousarray(inputs["L0_attn_x"].reshape(TOK, D))
    idb = np.eye(128, dtype=bf16)
    scv = _ffn_scales(inputs)
    w = {
        "w1t": np.ascontiguousarray(inputs["L0_ffn_w1_w"].T),
        "w2t": np.ascontiguousarray(inputs["L0_ffn_w2_w"].T),
        "h1t": np.ascontiguousarray(inputs["L0_ffn_h1_w"].T),
        "h2t": np.ascontiguousarray(inputs["L0_ffn_h2_w"].T),
        "w3t": np.ascontiguousarray(inputs["L0_ffn_w3_w"].T),
    }
    maps = []
    for c in range(NC):
        m = {"x": np.ascontiguousarray(x2d[c * TSH:(c + 1) * TSH]),
             "idb": idb, "idf": np.eye(128, dtype=np.float32), "scv": scv,
             "kn": KONST}
        m.update(w)
        maps.append(m)
    return maps




def build_B():
    nc = bass.Bass("TRN2", target_bir_lowering=False, debug=False)
    xnt_f = nc.dram_tensor("xnt_f", [NC * D, TSH], BF16, kind="ExternalInput").ap()
    bw_t = nc.dram_tensor("bw_t", [D, 128], F32, kind="ExternalInput").ap()
    a_sh = nc.dram_tensor("a_sh", [128, 1], F32, kind="ExternalInput").ap()
    h_d = nc.dram_tensor("h", [128, TOK], BF16, kind="ExternalOutput").ap()

    TCH = 1024
    NCH = S // TCH

    with ExitStack() as ctx:
        pr = Prog(nc, ctx)
        s0 = pr.sem("b0")
        s_ld = pr.sem("bld")
        s_mm = pr.sem("bmm")
        s_ev = pr.sem("bev")
        s_vz = pr.sem("bvz")
        s_ah = pr.sem("bah")
        s_oc = pr.sem("boc")
        s_st = pr.sem("bst")

        wfB = sb(ctx, nc, "bwf", (128, D), F32)
        bwb = sb(ctx, nc, "bwb", (128, D), BF16)
        av = sb(ctx, nc, "bav", (128, 2), F32)
        dec = sb(ctx, nc, "bdec", (128, 1), F32)
        xq = [sb(ctx, nc, f"bxq{i}", (128, 8 * TCH), BF16) for i in range(2)]
        bxi = [sb(ctx, nc, f"bbxi{i}", (128, 4 * TCH), F32) for i in range(2)]
        hh = sb(ctx, nc, "bh", (128, 4 * TCH + 4), F32)
        zb = sb(ctx, nc, "bz", (128, 8), F32)
        hob = sb(ctx, nc, "bho", (128, TCH), BF16)
        pX = [psum_bank(ctx, nc, f"bpX{i}") for i in range(2)]

        la = pr.dma("sync", av[:, 0:1], a_sh, s0)
        lw = pr.dma("sync", kt_dst(wfB[:], 8), rearr_kt(bw_t), s0)
        t1 = pr.op("vector", lambda q: q.tensor_scalar(
            out=av[:, 1:2], in0=av[:, 0:1], scalar1=10.0, scalar2=None,
            op0=ALU.min), waits=[(s0, la)], inc=s_vz)
        pr.op("scalar", lambda q: q.activation(out=dec[:], in_=av[:, 1:2],
                                               func=AF.Exp, scale=-1.0),
              waits=[(s_vz, t1)], inc=s_ah)
        cast_w = pr.op("vector", lambda q: q.tensor_copy(bwb[:], wfB[:]),
                       waits=[(s0, lw)], inc=s_vz)
        hz = pr.op("vector", lambda q: q.memset(hh[:, 0:4], 0.0), inc=s_vz)

        ah_base = s_ah.n
        bx_ready = [0, 0]
        bx_free = [0, 0]
        ev_pX = [0, 0]
        xq_free = [0, 0]
        st_done = 0
        oc_done = 0
        last_act = ah_base
        for sc in range(NCH):
            ci = sc % 2
            for b in range(B):
                t0 = sc * TCH
                r = b * 2 + (1 if t0 >= TSH else 0)
                off = t0 % TSH
                xb = b % 2
                lx = pr.dma("sync", kt_dst(xq[xb][:], 8),
                            rearr_kt(xnt_f[r * D:(r + 1) * D, off:off + TCH]),
                            s_ld, waits=[(s_mm, xq_free[xb])])
                for n2 in range(TCH // 512):
                    pi = n2 % 2
                    mm = None
                    for k in range(8):
                        w = []
                        if k == 0:
                            w = [(s_ld, lx), (s_vz, cast_w), (s_ev, ev_pX[pi])]
                        mm = pr.op("tensor", lambda q, k=k, n2=n2, pi=pi, xb=xb: q.matmul(
                            pX[pi][:], bwb[:, k * 128:(k + 1) * 128],
                            xq[xb][:, k * TCH + n2 * 512:k * TCH + (n2 + 1) * 512],
                            start=(k == 0), stop=(k == 7)),
                            waits=w, inc=s_mm if k == 7 else None)
                    bx_ready[ci] = pr.op("vector", lambda q, b=b, n2=n2, pi=pi, ci=ci: q.tensor_copy(
                        bxi[ci].rearrange("p (t f) -> p t f", f=4)[:, n2 * 512:(n2 + 1) * 512, b:b + 1],
                        pX[pi][:].rearrange("p (t f) -> p t f", f=1)),
                        waits=[(s_mm, mm), (s_vz, bx_free[ci])], inc=s_ev)
                    ev_pX[pi] = bx_ready[ci]
                xq_free[xb] = s_mm.n
            if sc > 0:
                cc = pr.op("vector", lambda q: q.tensor_copy(hh[:, 0:4], hh[:, 4 * TCH:4 * TCH + 4]),
                           waits=[(s_ah, last_act), (s_oc, oc_done)], inc=s_vz)
            else:
                cc = hz
            for t in range(TCH):
                zslot = t % 2
                w = [(s_ah, last_act)]
                if t == 0:
                    w += [(s_ev, bx_ready[ci]), (s_vz, cc), (s_oc, oc_done)]
                zt = pr.op("vector", lambda q, t=t, ci=ci, zslot=zslot: q.scalar_tensor_tensor(
                    out=zb[:, zslot * 4:(zslot + 1) * 4],
                    in0=hh[:, 4 * t:4 * t + 4], scalar=dec[:],
                    in1=bxi[ci][:, 4 * t:4 * t + 4],
                    op0=ALU.mult, op1=ALU.add),
                    waits=w, inc=s_vz)
                last_act = pr.op("scalar", lambda q, t=t, zslot=zslot: q.activation(
                    out=hh[:, 4 * (t + 1):4 * (t + 1) + 4],
                    in_=zb[:, zslot * 4:(zslot + 1) * 4], func=AF.Tanh),
                    waits=[(s_vz, zt)], inc=s_ah)
            bx_free[ci] = s_vz.n
            for b in range(B):
                oc_done = pr.op("vector", lambda q, b=b: q.tensor_copy(
                    hob[:].rearrange("p (t f) -> p t f", f=1),
                    hh[:, 4:4 + 4 * TCH].rearrange("p (t f) -> p t f", f=4)[:, :, b:b + 1]),
                    waits=[(s_ah, last_act), (s_st, st_done)], inc=s_oc)
                st_done = pr.dma("sync", h_d[:, b * S + sc * TCH:b * S + (sc + 1) * TCH],
                                 hob[:], s_st, waits=[(s_oc, oc_done)])
        pr.wait("gpsimd", [(s_st, s_st.n)])
        pr.finish()
    return nc


def host_prep_B(inputs, xnt_full):
    bw = inputs["L0_attn_B_w"]
    a = inputs["L0_attn_A"]
    maps = []
    for c in range(NC):
        maps.append({
            "xnt_f": xnt_full,
            "bw_t": np.ascontiguousarray(bw[c * 128:(c + 1) * 128, :].T),
            "a_sh": np.ascontiguousarray(a[c * 128:(c + 1) * 128, None]),
        })
    return maps




def build_C():
    nc = bass.Bass("TRN2", target_bir_lowering=False, debug=False)
    x_d = nc.dram_tensor("x", [TSH, D], F32, kind="ExternalInput").ap()
    g_d = nc.dram_tensor("g_tm", [TSH, D], F32, kind="ExternalInput").ap()
    h_d = nc.dram_tensor("h_sl", [D, TSH], BF16, kind="ExternalInput").ap()
    wo_d = nc.dram_tensor("wo_t", [D, D], F32, kind="ExternalInput").ap()
    bo_d = nc.dram_tensor("bo_v", [128, 8], F32, kind="ExternalInput").ap()
    idb_d = nc.dram_tensor("idb", [128, 128], BF16, kind="ExternalInput").ap()
    idf_d = nc.dram_tensor("idf", [128, 128], F32, kind="ExternalInput").ap()
    kn_d = nc.dram_tensor("kn", [128, 8], F32, kind="ExternalInput").ap()
    x1_d = nc.dram_tensor("x1", [TSH, D], F32, kind="ExternalOutput").ap()
    xn1t_d = nc.dram_tensor("xn1t", [D, TSH], BF16, kind="ExternalOutput").ap()

    with ExitStack() as ctx:
        pr = Prog(nc, ctx)
        s0 = pr.sem("c0")
        s_gl = pr.sem("cgl")
        s_cm = pr.sem("ccm")
        s_ce = pr.sem("cce")
        s_ct = pr.sem("cct")
        s_cv = pr.sem("ccv")

        idb = sb(ctx, nc, "idbs", (128, 128), BF16)
        idf = sb(ctx, nc, "idfs", (128, 128), F32)
        kn = sb(ctx, nc, "kns", (128, 8), F32)
        bo = sb(ctx, nc, "bos", (128, 8), F32)
        wof = sb(ctx, nc, "cwof", (128, 8 * D), F32)
        wob = sb(ctx, nc, "cwob", (128, 8 * D), BF16)
        hsb = sb(ctx, nc, "chsb", (128, 8 * TSH), BF16)
        gsb = [sb(ctx, nc, f"cgs{i}", (128, D), F32) for i in range(2)]
        sblk = sb(ctx, nc, "csblk", (128, 128), F32)
        sgb = sb(ctx, nc, "csgb", (128, 128), F32)
        big = sb(ctx, nc, "cbig", (128, 8 * TSH), BF16)
        xnt_sb = [big[:, j * TSH:(j + 1) * TSH] for j in range(8)]
        pm = [psum_bank(ctx, nc, f"cpm{i}") for i in range(2)]
        pT2 = psum_bank(ctx, nc, "cpT2")

        for nm, dst, ap in [("idb", idb, idb_d), ("idf", idf, idf_d),
                            ("kn", kn, kn_d), ("bo", bo, bo_d)]:
            pr.dma("sync", dst[:], ap, s0)
        lw = pr.dma("sync", kt_dst(wof[:], 8), rearr_kt(wo_d), s0)
        cwo = pr.op("vector", lambda q: q.tensor_copy(wob[:], wof[:]),
                    waits=[(s0, lw)], inc=s_cv)
        lh = pr.dma("sync", kt_dst(hsb[:], 8), rearr_kt(h_d), s0)
        pr.wait("scalar", [(s0, s0.n)])
        pr.wait("tensor", [(s0, s0.n)])

        wov = wob.rearrange("p (a c) -> p a c", a=8)
        hsv = hsb.rearrange("p (a c) -> p a c", a=8)
        state = {"ev_pm": [0, 0], "tr": 0, "sgf": 0, "gld": [0, 0], "gfree": [0, 0]}

        def residual(i, xt, waits):
            bi = i % 2
            state["gld"][bi] = pr.dma("sync", gsb[bi][:], g_d[i * 128:(i + 1) * 128, :],
                                      s_gl, waits=[(s_cv, state["gfree"][bi])])
            add_t = 0
            for m in range(8):
                pi = m % 2
                mm = None
                for k in range(8):
                    w = []
                    if k == 0:
                        w = [(s0, lh), (s_cv, cwo), (s_ce, state["ev_pm"][pi])]
                    mm = pr.op("tensor", lambda q, k=k, m=m, pi=pi, i=i: q.matmul(
                        pm[pi][:, 0:128], wov[:, k, m * 128:(m + 1) * 128],
                        hsv[:, k, i * 128:(i + 1) * 128],
                        start=(k == 0), stop=(k == 7)),
                        waits=w, inc=s_cm if k == 7 else None)
                state["ev_pm"][pi] = pr.op("vector", lambda q, m=m, pi=pi: q.tensor_scalar(
                    out=sblk[:], in0=pm[pi][:, 0:128], scalar1=bo[:, m:m + 1],
                    scalar2=None, op0=ALU.add),
                    waits=[(s_cm, mm), (s_ct, state["tr"])], inc=s_ce)
                state["tr"] = pr.op("tensor", lambda q: q.transpose(
                    pT2[:, 0:128], sblk[:], idf[:]),
                    waits=[(s_ce, state["ev_pm"][pi]), (s_cv, state["sgf"])], inc=s_ct)
                state["sgf"] = pr.op("vector", lambda q, m=m, bi=bi: q.tensor_tensor(
                    out=sgb[:], in0=pT2[:, 0:128],
                    in1=gsb[bi][:, m * 128:(m + 1) * 128], op=ALU.mult),
                    waits=[(s_ct, state["tr"]), (s_gl, state["gld"][bi])], inc=s_cv)
                add_t = pr.op("vector", lambda q, m=m, t=xt: q.tensor_tensor(
                    out=t[:, m * 128:(m + 1) * 128], in0=t[:, m * 128:(m + 1) * 128],
                    in1=sgb[:], op=ALU.add),
                    waits=[(s_cv, state["sgf"])] + (waits if m == 0 else []), inc=s_cv)
                state["sgf"] = add_t
            state["gfree"][bi] = add_t
            return [(s_cv, add_t)]

        ln_st = emit_ln_phase(pr, ctx, nc, x_d, xnt_sb, xn1t_d, idb, TSH // 128,
                              "ln", residual=residual, x1_out=x1_d, kn=kn)
        pr.wait("gpsimd", [(ln_st, ln_st.n)])
        pr.finish()
    return nc


def host_prep_C(inputs, resA, resB):
    x2d = inputs["L0_attn_x"].reshape(TOK, D)
    h_full = np.concatenate([resB[c]["h"] for c in range(NC)], axis=0)
    maps = []
    for c in range(NC):
        maps.append({
            "x": np.ascontiguousarray(x2d[c * TSH:(c + 1) * TSH]),
            "g_tm": resA[c]["g_tm"],
            "h_sl": np.ascontiguousarray(h_full[:, c * TSH:(c + 1) * TSH]),
            "wo_t": np.ascontiguousarray(inputs["L0_attn_Wo"].T),
            "bo_v": np.ascontiguousarray(inputs["L0_attn_bo"].reshape(8, 128).T),
            "idb": np.eye(128, dtype=bf16),
            "idf": np.eye(128, dtype=np.float32),
            "kn": KONST,
        })
    return maps




MC = 1024
NMC = S // MC
NG = MC // 128
DBG_D = False
MXV = 1


def build_D():
    nc = bass.Bass("TRN2", target_bir_lowering=False, debug=False)
    dr = {}
    for nm, shp, dt in [
        ("xn1t_b", (D, S), BF16), ("wq", (D, 512), F32), ("wk", (D, 512), F32),
        ("wv", (D, 512), F32), ("wb2", (D, 2), F32), ("qcw", (128, 16), F32),
        ("kcw", (128, 16), F32), ("vcw", (128, 16), F32),
        ("firs", (128, 12), F32), ("firl", (128, 124), F32),
        ("gw1h", (D, GH), F32), ("gw1s", (3072, GH), F32),
        ("gw1b", (D, GH), F32), ("gw2", (GH, 4), F32),
        ("gb1v", (128, 4), F32), ("gb2b", (128, 4), F32),
        ("tmpb", (128, 2), F32), ("flrb", (128, 8), F32),
        ("onormv", (128, 2), F32), ("mneg", (128, 128), F32),
        ("miu", (32, 128), F32), ("idb", (128, 128), BF16),
        ("idf", (128, 128), F32),
        ("ones1f", (1, 128), F32), ("ones128b", (128, 1), BF16),
        ("ones128f", (128, 1), F32), ("kn", (128, 8), F32),
    ]:
        dr[nm] = nc.dram_tensor(nm, list(shp), dt, kind="ExternalInput").ap()
    oT_d = nc.dram_tensor("oT", [512, S], BF16, kind="ExternalOutput").ap()
    dmx_d = nc.dram_tensor("dmx", [1, 1024], F32).ap()
    dbg = {}
    if DBG_D:
        for nm, shp, dt in [("d_qn", (256, MC), BF16), ("d_kn", (256, MC), BF16),
                            ("d_kb", (256, MC), BF16), ("d_vb", (256, MC), BF16),
                            ("d_vf", (256, MC), BF16),
                            ("d_fs", (256, MC), BF16), ("d_fl", (256, MC), BF16),
                            ("d_do", (256, MC), BF16), ("d_st", (12, MC), F32),
                            ("d_w8", (128, 32), F32), ("d_hm", (128, 4 * MC), BF16),
                            ("d_brow", (1, MC), F32), ("d_o0", (128, MC), F32),
                            ("d_kbn", (128, 256), BF16), ("d_knn", (128, 256), BF16),
                            ("d_vbn", (128, 256), BF16), ("d_at", (128, 128), BF16),
                            ("d_a2", (128, 128), BF16), ("d_ut", (128, 256), BF16),
                            ("d_wt", (128, 256), BF16), ("d_attg", (32, 128), BF16),
                            ("d_ub", (32, 256), BF16), ("d_ss", (256, 256), BF16),
                            ("d_och", (32, 256), BF16), ("d_att2", (32, 256), F32)]:
            dbg[nm] = nc.dram_tensor(nm, list(shp), dt, kind="ExternalOutput").ap()

    with ExitStack() as ctx:
        pr = Prog(nc, ctx)
        s0 = pr.sem("d0")
        s_ld = pr.sem("dld")
        s_mm = pr.sem("dmm")
        s_ev = pr.sem("dev")
        s_ac = pr.sem("dac")
        s_gp = pr.sem("dgp")
        s_st = pr.sem("dst")
        s_hp = pr.sem("dhp")

        cst = {}
        for nm, shp, dt in [
            ("idb", (128, 128), BF16), ("idf", (128, 128), F32),
            ("kn", (128, 8), F32),
            ("mneg", (128, 128), F32), ("miu", (32, 128), F32),
            ("ones1f", (1, 128), F32), ("ones128b", (128, 1), BF16),
            ("ones128f", (128, 1), F32), ("gb1v", (128, 4), F32),
            ("gb2b", (128, 4), F32), ("tmpb", (128, 2), F32),
            ("flrb", (128, 8), F32), ("onormv", (128, 2), F32),
            ("qcw", (128, 16), F32), ("kcw", (128, 16), F32),
            ("vcw", (128, 16), F32), ("firs", (128, 12), F32),
            ("firl", (128, 124), F32),
        ]:
            cst[nm] = sb(ctx, nc, nm + "_s", shp, dt)
        idb = cst["idb"]
        wbb = sb(ctx, nc, "wbb_s", (128, 16), BF16)
        gw1hb = sb(ctx, nc, "gw1h_b", (128, 8 * GH), BF16)
        gw1bb = sb(ctx, nc, "gw1b_b", (128, 8 * GH), BF16)
        gw2b = sb(ctx, nc, "gw2_b", (128, 16), BF16)
        statw = sb(ctx, nc, "statw", (12, GH), F32)
        statwb = sb(ctx, nc, "statwb", (12, GH), BF16)
        stg = sb(ctx, nc, "stg", (128, GH), F32)
        xq = sb(ctx, nc, "xq", (128, 8 * MC), BF16)
        wproj = sb(ctx, nc, "wproj", (128, 8 * 256), BF16)
        pre = [sb(ctx, nc, f"pre{h}", (128, MC + 3), F32) for h in range(2)]
        tmp32 = [sb(ctx, nc, f"tmp32_{i}", (128, MC), F32) for i in range(2)]
        knf = [sb(ctx, nc, f"knf{h}", (128, MC), F32) for h in range(2)]
        vF = [sb(ctx, nc, f"vF{h}", (128, MC + 30), F32) for h in range(2)]
        halo = {w: [sb(ctx, nc, f"halo_{w}{h}", (128, 3), F32) for h in range(2)]
                for w in "qkv"}
        qnb = [sb(ctx, nc, f"qnb{h}", (128, MC), BF16) for h in range(2)]
        knb = [sb(ctx, nc, f"knb{h}", (128, MC), BF16) for h in range(2)]
        kbb = [sb(ctx, nc, f"kbb{h}", (128, MC), BF16) for h in range(2)]
        vbb = [sb(ctx, nc, f"vbb{h}", (128, MC), BF16) for h in range(2)]
        vfb = [sb(ctx, nc, f"vfb{h}", (128, MC), BF16) for h in range(2)]
        fir32 = [sb(ctx, nc, f"fir32_{h}", (128, MC), F32) for h in range(2)]
        fsb = [sb(ctx, nc, f"fsb{h}", (128, MC), BF16) for h in range(2)]
        flb = [sb(ctx, nc, f"flb{h}", (128, MC), BF16) for h in range(2)]
        doT = [sb(ctx, nc, f"doT{h}", (128, MC), BF16) for h in range(2)]
        sqb = sb(ctx, nc, "sqb", (128, 512), BF16)
        bcast = sb(ctx, nc, "bcast", (128, MC), F32)
        betaB = sb(ctx, nc, "betaB", (128, MC), F32)
        rrow = sb(ctx, nc, "rrow", (1, MC), F32)
        srow = sb(ctx, nc, "srow", (1, MC), F32)
        brow = sb(ctx, nc, "brow", (1, MC), F32)
        kbn = sb(ctx, nc, "kbn", (128, 256), BF16)
        knn = sb(ctx, nc, "knn", (128, 256), BF16)
        vbn = sb(ctx, nc, "vbn", (128, 256), BF16)
        pt_sb = [sb(ctx, nc, f"pt{j}", (128, 128), BF16) for j in range(5)]
        pn_sb = [sb(ctx, nc, f"pn{j}", (128, 128), BF16) for j in range(4)]
        ucur = [sb(ctx, nc, f"ucur{i}", (128, 256), BF16) for i in range(2)]
        Ut = sb(ctx, nc, "Ut", (128, 256), BF16)
        Wt = sb(ctx, nc, "Wt", (128, 256), BF16)
        WT = [sb(ctx, nc, f"WTh{i}", (128, 128), BF16) for i in range(2)]
        ATTg = sb(ctx, nc, "ATTg", (32, 128), BF16)
        Uc = [sb(ctx, nc, f"Uc{i}", (32, 256), BF16) for i in range(4)]
        knc = [sb(ctx, nc, f"knc{i}", (32, 256), BF16) for i in range(4)]
        Ssb = [sb(ctx, nc, f"Ssb{i}", (128, 256), BF16) for i in range(2)]
        ub = sb(ctx, nc, "ub", (32, 256), BF16)
        ochb = sb(ctx, nc, "ochb", (32, 256), BF16)
        dbgb = sb(ctx, nc, "dbgb", (32, 256), BF16) if DBG_D else None
        dbgb2 = sb(ctx, nc, "dbgb2", (32, 256), F32) if DBG_D else None
        strow = sb(ctx, nc, "strow", (12, MC), F32)
        strowb = sb(ctx, nc, "strowb", (12, MC), BF16)
        hmid = sb(ctx, nc, "hmid", (128, 4 * MC), BF16)
        lgt = sb(ctx, nc, "lgt", (128, 8), F32)
        wfin = sb(ctx, nc, "wfin", (128, 4), F32)
        wfin8 = sb(ctx, nc, "wfin8", (128, 32), F32)
        invt = sb(ctx, nc, "invt", (128, 2), F32)
        flc = sb(ctx, nc, "flc", (128, 8), F32)
        o32 = [sb(ctx, nc, f"o32_{h}", (128, MC), F32) for h in range(2)]
        obf = sb(ctx, nc, "obf", (128, MC), BF16)
        mx8col = sb(ctx, nc, "mx8col", (128, 8), BF16)
        mrow8f = sb(ctx, nc, "mrow8f", (8, 128), F32)
        pQ = [psum_bank(ctx, nc, f"dpQ{i}") for i in range(2)]
        pS = [psum_bank(ctx, nc, f"dpS{i}") for i in range(2)]
        pW = [psum_bank(ctx, nc, f"dpW{i}") for i in range(2)]
        pT = psum_bank(ctx, nc, "dpT", BF16)
        pSm = psum_bank(ctx, nc, "dpSm")

        xqv = xq.rearrange("p (a c) -> p a c", a=8)
        wpv = wproj.rearrange("p (a c) -> p a c", a=8)
        wbv = wbb.rearrange("p (a c) -> p a c", a=8)
        g1hv = gw1hb.rearrange("p (a c) -> p a c", a=8)
        g1bv = gw1bb.rearrange("p (a c) -> p a c", a=8)
        g2v = gw2b.rearrange("p (a c) -> p a c", a=4)

        for nm in cst:
            pr.dma("sync", cst[nm][:], dr[nm], s0)
        pr.dma("gpsimd", wbv, rearr_kt(dr["wb2"]), s_gp)
        pr.dma("gpsimd", kt_dst(gw1hb[:], 8), rearr_kt(dr["gw1h"]), s_gp)
        pr.dma("gpsimd", kt_dst(gw1bb[:], 8), rearr_kt(dr["gw1b"]), s_gp)
        pr.dma("gpsimd", g2v, rearr_kt(dr["gw2"]), s_gp)
        gp_init = s_gp.n
        ev_stg = 0
        for i in range(12):
            mm = None
            for kk in range(2):
                lt = pr.dma("sync", stg[:],
                            dr["gw1s"][i * 256 + kk * 128:i * 256 + (kk + 1) * 128, :],
                            s_ld, waits=[(s_mm, s_mm.n)])
                mm = pr.op("tensor", lambda q, kk=kk: q.matmul(
                    pSm[0:1, :], cst["ones128f"][:], stg[:],
                    start=(kk == 0), stop=(kk == 1)),
                    waits=[(s_ld, lt), (s0, s0.n), (s_ev, ev_stg)], inc=s_mm)
            ev_stg = pr.op("vector", lambda q: q.tensor_copy(srow[0:1, 0:GH], pSm[0:1, :]),
                           waits=[(s_mm, mm)], inc=s_ev)
            pr.dma("scalar", statw[i:i + 1, :], srow[0:1, 0:GH], s_hp,
                   waits=[(s_ev, ev_stg)])
        pr.op("vector", lambda q: q.tensor_copy(statwb[:], statw[:]),
              waits=[(s_hp, s_hp.n), (s_gp, gp_init)], inc=s_ev)
        it0 = pr.op("vector", lambda q: q.tensor_scalar(
            out=invt[:], in0=cst["tmpb"][:], scalar1=0.2, scalar2=10.0,
            op0=ALU.max, op1=ALU.min), waits=[(s0, s0.n)], inc=s_ev)
        pr.op("vector", lambda q: q.reciprocal(invt[:], invt[:]),
              waits=[(s_ev, it0)], inc=s_ev)
        pr.op("vector", lambda q: q.tensor_scalar(
            out=flc[:], in0=cst["flrb"][:], scalar1=1e-7, scalar2=0.1,
            op0=ALU.max, op1=ALU.min), waits=[(s0, s0.n)], inc=s_ev)
        for e in ("tensor", "scalar", "vector", "gpsimd"):
            pr.wait(e, [(s0, s0.n), (s_gp, gp_init)])

        def bcast_row(row, dst, extra_waits=()):
            last = 0
            for c2 in range(MC // 512):
                mm = pr.op("tensor", lambda q, c2=c2, row=row: q.matmul(
                    pQ[c2 % 2][:], cst["ones1f"][:], row[0:1, c2 * 512:(c2 + 1) * 512],
                    start=True, stop=True),
                    waits=list(extra_waits) + [(s_ev, s_ev.n), (s_ac, s_ac.n)],
                    inc=s_mm)
                last = pr.op("vector", lambda q, c2=c2, dst=dst: q.tensor_copy(
                    dst[:, c2 * 512:(c2 + 1) * 512], pQ[c2 % 2][:]),
                    waits=[(s_mm, mm)], inc=s_ev)
            return last

        def proj(wname, hl):
            lw = pr.dma("gpsimd", kt_dst(wproj[:], 8),
                        rearr_kt(dr[wname][:, hl * 256:(hl + 1) * 256]),
                        s_gp, waits=[(s_mm, s_mm.n)])
            ev = 0
            for hf in range(2):
                for c2 in range(MC // 512):
                    pi = c2 % 2
                    mm = None
                    for k in range(8):
                        w = [(s_gp, lw), (s_ld, lx_cur[0]),
                             (s_ev, s_ev.n), (s_ac, s_ac.n)] if k == 0 else []
                        mm = pr.op("tensor", lambda q, k=k, c2=c2, hf=hf, pi=pi: q.matmul(
                            pQ[pi][:], wpv[:, k, hf * 128:(hf + 1) * 128],
                            xqv[:, k, c2 * 512:(c2 + 1) * 512],
                            start=(k == 0), stop=(k == 7)),
                            waits=w, inc=s_mm if k == 7 else None)
                    ev = pr.op("vector", lambda q, hf=hf, c2=c2, pi=pi: q.tensor_copy(
                        pre[hf][:, 3 + c2 * 512:3 + (c2 + 1) * 512], pQ[pi][:]),
                        waits=[(s_mm, mm)], inc=s_ev)
            return ev

        def halo_io(w, hf, ev):
            h1 = pr.op("vector", lambda q, w=w, hf=hf: q.tensor_copy(
                pre[hf][:, 0:3], halo[w][hf][:]), waits=[(s_ev, ev)], inc=s_ev)
            h2 = pr.op("vector", lambda q, w=w, hf=hf: q.tensor_copy(
                halo[w][hf][:], pre[hf][:, MC:MC + 3]),
                waits=[(s_ev, h1)], inc=s_ev)
            return h2

        def conv4(pre_t, wtile, col0, waits):
            a, b_ = tmp32
            tgt = pr.op("vector", lambda q, col0=col0, pre_t=pre_t, wtile=wtile: q.tensor_scalar(
                out=a[:], in0=pre_t[:, 0:MC], scalar1=wtile[:, col0:col0 + 1],
                scalar2=None, op0=ALU.mult),
                waits=waits + [(s_ac, s_ac.n)], inc=s_ev)
            bufs = [a, b_]
            for j in range(1, 4):
                dst, srcb = bufs[j % 2], bufs[(j + 1) % 2]
                tgt = pr.op("vector", lambda q, j=j, dst=dst, srcb=srcb, col0=col0,
                            pre_t=pre_t, wtile=wtile: q.scalar_tensor_tensor(
                    out=dst[:], in0=pre_t[:, j:j + MC],
                    scalar=wtile[:, col0 + j:col0 + j + 1],
                    in1=srcb[:], op0=ALU.mult, op1=ALU.add),
                    waits=[(s_ev, tgt)], inc=s_ev)
            return tgt, bufs[1]

        def l2norm(src_aps, dstbf, keep32):
            for c2 in range(MC // 512):
                mm = None
                for hf in range(2):
                    sq = pr.op("scalar", lambda q, hf=hf, c2=c2: q.activation(
                        out=sqb[:], in_=src_aps[hf][:, c2 * 512:(c2 + 1) * 512],
                        func=AF.Square),
                        waits=[(s_mm, s_mm.n), (s_ev, s_ev.n)], inc=s_ac)
                    mm = pr.op("tensor", lambda q, hf=hf: q.matmul(
                        pSm[0:1, :], cst["ones128b"][:], sqb[:],
                        start=(hf == 0), stop=(hf == 1)),
                        waits=[(s_ac, sq)] + ([(s_ev, s_ev.n)] if hf == 0 else []),
                        inc=s_mm)
                pr.op("vector", lambda q, c2=c2: q.tensor_scalar(
                    out=rrow[0:1, c2 * 512:(c2 + 1) * 512], in0=pSm[0:1, :],
                    scalar1=1e-6, scalar2=None, op0=ALU.add),
                    waits=[(s_mm, mm)], inc=s_ev)
            a1 = pr.op("scalar", lambda q: q.activation(out=srow[0:1, :],
                                                        in_=rrow[0:1, :], func=AF.Ln),
                       waits=[(s_ev, s_ev.n)], inc=s_ac)
            a2 = pr.op("scalar", lambda q: q.activation(out=rrow[0:1, :],
                                                        in_=srow[0:1, :],
                                                        func=AF.Exp, scale=-0.5),
                       waits=[(s_ac, a1)], inc=s_ac)
            bcast_row(rrow, bcast, extra_waits=[(s_ac, a2)])
            for hf in range(2):
                if keep32 is not None:
                    t1 = pr.op("vector", lambda q, hf=hf: q.tensor_tensor(
                        out=keep32[hf][:, 0:MC], in0=src_aps[hf][:, 0:MC],
                        in1=bcast[:], op=ALU.mult),
                        waits=[(s_ev, s_ev.n)], inc=s_ev)
                    pr.op("vector", lambda q, hf=hf: q.tensor_copy(
                        dstbf[hf][:], keep32[hf][:, 0:MC]),
                        waits=[(s_ev, t1)], inc=s_ev)
                else:
                    pr.op("vector", lambda q, hf=hf: q.tensor_tensor(
                        out=dstbf[hf][:], in0=src_aps[hf][:, 0:MC],
                        in1=bcast[:], op=ALU.mult),
                        waits=[(s_ev, s_ev.n)], inc=s_ev)

        def fir_pass(taps, wtile, ncols, pad, dstbf, hl):
            for hf in range(2):
                col0 = (hl * 2 + hf) * ncols
                bufs = [fir32[hf], tmp32[1]]
                tgt = pr.op("vector", lambda q, hf=hf, col0=col0, wtile=wtile,
                            pad=pad: q.tensor_scalar(
                    out=fir32[hf][:], in0=vF[hf][:, 30 - pad:30 - pad + MC],
                    scalar1=wtile[:, col0:col0 + 1], scalar2=None, op0=ALU.mult),
                    waits=[(s_ev, s_ev.n)], inc=s_ev)
                for j in range(1, taps):
                    dst, srcb = bufs[j % 2], bufs[(j + 1) % 2]
                    tgt = pr.op("vector", lambda q, hf=hf, j=j, dst=dst, srcb=srcb,
                                col0=col0, wtile=wtile, pad=pad: q.scalar_tensor_tensor(
                        out=dst[:], in0=vF[hf][:, 30 - pad + j:30 - pad + j + MC],
                        scalar=wtile[:, col0 + j:col0 + j + 1],
                        in1=srcb[:], op0=ALU.mult, op1=ALU.add),
                        waits=[(s_ev, tgt)], inc=s_ev)
                fin = bufs[(taps - 1) % 2]
                pr.op("vector", lambda q, hf=hf, fin=fin: q.tensor_copy(
                    dstbf[hf][:], fin[:]), waits=[(s_ev, tgt)], inc=s_ev)

        lx_cur = [0]
        mxs = {"ev": [0, 0], "tr": 0, "evac": 0, "d2": 0}

        for hl in range(2):
            z0 = pr.op("vector", lambda q: q.memset(Ssb[0][:], 0.0),
                       waits=[(s_mm, s_mm.n)], inc=s_ev)
            pr.op("vector", lambda q: q.memset(Ssb[1][:], 0.0),
                  waits=[(s_ev, z0)], inc=s_ev)
            for w in "qkv":
                for hf in range(2):
                    pr.op("vector", lambda q, w=w, hf=hf: q.memset(halo[w][hf][:], 0.0),
                          inc=s_ev)
            for hf in range(2):
                pr.op("vector", lambda q, hf=hf: q.memset(vF[hf][:, 0:30], 0.0),
                      inc=s_ev)
            for mc in range(NMC):
                pr.wait("tensor", [(s_ev, mxs["evac"])])
                lx_cur[0] = pr.dma("sync", kt_dst(xq[:], 8),
                                   rearr_kt(dr["xn1t_b"][:, mc * MC:(mc + 1) * MC]),
                                   s_ld, waits=[(s_mm, s_mm.n)])
                ev = proj("wq", hl)
                for hf in range(2):
                    h2 = halo_io("q", hf, ev)
                    tgt, fin = conv4(pre[hf], cst["qcw"], (hl * 2 + hf) * 4,
                                     [(s_ev, h2)])
                    pr.op("scalar", lambda q, hf=hf, fin=fin: q.activation(
                        out=pre[hf][:, 3:3 + MC], in_=fin[:], func=AF.Silu),
                        waits=[(s_ev, tgt)], inc=s_ac)
                l2norm([pre[0][:, 3:3 + MC + 0].rearrange("p c -> p c") if False
                        else pre[0][:, 3:3 + MC],
                        pre[1][:, 3:3 + MC]], qnb, None)
                ev = proj("wk", hl)
                for hf in range(2):
                    h2 = halo_io("k", hf, ev)
                    tgt, fin = conv4(pre[hf], cst["kcw"], (hl * 2 + hf) * 4,
                                     [(s_ev, h2)])
                    pr.op("scalar", lambda q, hf=hf, fin=fin: q.activation(
                        out=knf[hf][:], in_=fin[:], func=AF.Silu),
                        waits=[(s_ev, tgt)], inc=s_ac)
                pr.wait("vector", [(s_ac, s_ac.n)])
                l2norm([knf[0][:], knf[1][:]], knb, [knf[0][:], knf[1][:]])
                bmm = None
                for c2 in range(MC // 512):
                    for k in range(8):
                        w = [(s_ld, lx_cur[0]), (s_ev, s_ev.n), (s_ac, s_ac.n)] \
                            if k == 0 else []
                        bmm = pr.op("tensor", lambda q, k=k, c2=c2, hl=hl: q.matmul(
                            pSm[0:1, :], wbv[:, k, hl:hl + 1],
                            xqv[:, k, c2 * 512:(c2 + 1) * 512],
                            start=(k == 0), stop=(k == 7)),
                            waits=w, inc=s_mm if k == 7 else None)
                    pr.op("scalar", lambda q, c2=c2: q.activation(
                        out=brow[0:1, c2 * 512:(c2 + 1) * 512], in_=pSm[0:1, :],
                        func=AF.Sigmoid), waits=[(s_mm, bmm)], inc=s_ac)
                bcast_row(brow, betaB, extra_waits=[(s_ac, s_ac.n)])
                for hf in range(2):
                    pr.op("vector", lambda q, hf=hf: q.tensor_tensor(
                        out=kbb[hf][:], in0=knf[hf][:], in1=betaB[:], op=ALU.mult),
                        waits=[(s_ev, s_ev.n)], inc=s_ev)
                ev = proj("wv", hl)
                for hf in range(2):
                    if mc > 0:
                        hv = pr.op("vector", lambda q, hf=hf: q.tensor_copy(
                            vF[hf][:, 0:30], vF[hf][:, MC:MC + 30]),
                            waits=[(s_ev, s_ev.n)], inc=s_ev)
                    h2 = halo_io("v", hf, ev)
                    tgt, fin = conv4(pre[hf], cst["vcw"], (hl * 2 + hf) * 4,
                                     [(s_ev, h2)])
                    sv_ = pr.op("scalar", lambda q, hf=hf, fin=fin: q.activation(
                        out=vF[hf][:, 30:30 + MC], in_=fin[:], func=AF.Silu),
                        waits=[(s_ev, tgt)], inc=s_ac)
                    pr.op("vector", lambda q, hf=hf: q.tensor_tensor(
                        out=vbb[hf][:], in0=vF[hf][:, 30:30 + MC], in1=betaB[:],
                        op=ALU.mult), waits=[(s_ac, sv_)], inc=s_ev)
                    pr.op("vector", lambda q, hf=hf: q.tensor_copy(
                        vfb[hf][:], vF[hf][:, 30:30 + MC]),
                        waits=[(s_ac, sv_)], inc=s_ev)
                fir_pass(3, cst["firs"], 3, 2, fsb, hl)
                fir_pass(31, cst["firl"], 31, 30, flb, hl)
                for g in range(NG):
                    gc = slice(g * 128, (g + 1) * 128)
                    for srcb, dstn in ((kbb, kbn), (knb, knn), (vbb, vbn)):
                        for hf in range(2):
                            tt = pr.op("tensor", lambda q, srcb=srcb, hf=hf, gc=gc: q.transpose(
                                pT[:, 0:128], srcb[hf][:, gc], idb[:]),
                                waits=[(s_ev, s_ev.n)], inc=s_mm)
                            pr.op("vector", lambda q, dstn=dstn, hf=hf: q.tensor_copy(
                                dstn[:, hf * 128:(hf + 1) * 128], pT[:, 0:128]),
                                waits=[(s_mm, tt)], inc=s_ev)
                    mm = None
                    for hf in range(2):
                        mm = pr.op("tensor", lambda q, hf=hf, gc=gc: q.matmul(
                            pW[0][:, 0:128], knb[hf][:, gc], kbb[hf][:, gc],
                            start=(hf == 0), stop=(hf == 1)),
                            waits=[(s_ev, s_ev.n)] if hf == 0 else [], inc=s_mm)
                    e0 = pr.op("vector", lambda q: q.tensor_tensor(
                        out=pt_sb[0][:], in0=pW[0][:, 0:128], in1=cst["mneg"][:],
                        op=ALU.mult), waits=[(s_mm, mm)], inc=s_ev)
                    tt = pr.op("tensor", lambda q: q.transpose(
                        pT[:, 0:128], pt_sb[0][:], idb[:]),
                        waits=[(s_ev, e0)], inc=s_mm)
                    pr.op("vector", lambda q: q.tensor_copy(pn_sb[0][:], pT[:, 0:128]),
                          waits=[(s_mm, tt)], inc=s_ev)
                    for j in range(1, 5):
                        mm = pr.op("tensor", lambda q, j=j: q.matmul(
                            pW[j % 2][:, 0:128], pn_sb[j - 1][:], pt_sb[j - 1][:],
                            start=True, stop=True),
                            waits=[(s_ev, s_ev.n)], inc=s_mm)
                        ej = pr.op("vector", lambda q, j=j: q.tensor_copy(
                            pt_sb[j][:], pW[j % 2][:, 0:128]),
                            waits=[(s_mm, mm)], inc=s_ev)
                        if j < 4:
                            tt = pr.op("tensor", lambda q, j=j: q.transpose(
                                pT[:, 0:128], pt_sb[j][:], idb[:]),
                                waits=[(s_ev, ej)], inc=s_mm)
                            pr.op("vector", lambda q, j=j: q.tensor_copy(
                                pn_sb[j][:], pT[:, 0:128]),
                                waits=[(s_mm, tt)], inc=s_ev)
                    for srcn, dst in ((vbn, Ut), (kbn, Wt)):
                        curl = pr.op("vector", lambda q, srcn=srcn: q.tensor_copy(
                            ucur[0][:], srcn[:]), waits=[(s_ev, s_ev.n)], inc=s_ev)
                        cur = ucur[0]
                        for ji, j in enumerate((4, 3, 2, 1, 0)):
                            nx = ucur[(ji + 1) % 2] if ji < 4 else dst
                            mm = pr.op("tensor", lambda q, j=j, cur=cur: q.matmul(
                                pW[0][:, 0:256], pt_sb[j][:], cur[:],
                                start=True, stop=True),
                                waits=[(s_ev, curl)], inc=s_mm)
                            curl = pr.op("vector", lambda q, cur=cur, nx=nx: q.tensor_tensor(
                                out=nx[:], in0=pW[0][:, 0:256], in1=cur[:], op=ALU.add),
                                waits=[(s_mm, mm)], inc=s_ev)
                            cur = nx
                    for hf in range(2):
                        tt = pr.op("tensor", lambda q, hf=hf: q.transpose(
                            pT[:, 0:128], Wt[:, hf * 128:(hf + 1) * 128], idb[:]),
                            waits=[(s_ev, s_ev.n)], inc=s_mm)
                        pr.op("vector", lambda q, hf=hf: q.tensor_copy(
                            WT[hf][:], pT[:, 0:128]), waits=[(s_mm, tt)], inc=s_ev)
                    am = None
                    for cc in range(4):
                        for hf in range(2):
                            am = pr.op("tensor", lambda q, cc=cc, hf=hf, g=g: q.matmul(
                                pSm[0:32, cc * 32:(cc + 1) * 32],
                                knb[hf][:, g * 128 + cc * 32:g * 128 + (cc + 1) * 32],
                                qnb[hf][:, g * 128 + cc * 32:g * 128 + (cc + 1) * 32],
                                start=(hf == 0), stop=(hf == 1)),
                                waits=[(s_ev, s_ev.n), (s_ac, s_ac.n)]
                                if (cc == 0 and hf == 0) else [],
                                inc=s_mm)
                    ea = pr.op("vector", lambda q: q.tensor_tensor(
                        out=ATTg[:], in0=pSm[0:32, 0:128], in1=cst["miu"][:],
                        op=ALU.mult), waits=[(s_mm, am)], inc=s_ev)
                    if DBG_D and hl == 0 and mc == 0 and g == 0:
                        for nm, t in [("d_kbn", kbn), ("d_knn", knn), ("d_vbn", vbn),
                                      ("d_at", pt_sb[0]), ("d_a2", pt_sb[1]),
                                      ("d_ut", Ut), ("d_wt", Wt), ("d_attg", ATTg)]:
                            pr.dma("sync", dbg[nm], t[:], s_st,
                                   waits=[(s_ev, s_ev.n)])
                    hU = [0] * 4
                    hK = [0] * 4
                    for cc in range(4):
                        hU[cc] = pr.dma("scalar", Uc[cc][:],
                                        Ut[cc * 32:(cc + 1) * 32, :], s_hp,
                                        waits=[(s_ev, s_ev.n)])
                        hK[cc] = pr.dma("scalar", knc[cc][:],
                                        knn[cc * 32:(cc + 1) * 32, :], s_hp,
                                        waits=[(s_ev, s_ev.n)])
                    prep_done = s_ev.n
                    for cc in range(4):
                        gchunk = (hl * 0) + mc * MC // 32 + g * 4 + cc
                        first_s = (mc == 0 and g == 0 and cc == 0)
                        base = cc * 32
                        sc = [0, 0]
                        if not first_s:
                            for kk in range(2):
                                sc[kk] = pr.op("vector", lambda q, kk=kk: q.tensor_copy(
                                    Ssb[kk][:], pS[kk][:, 0:256]),
                                    waits=[(s_mm, s_mm.n)], inc=s_ev)
                        scmax = max(sc + [prep_done])
                        if DBG_D and hl == 0 and mc == 0 and g == 0 and cc == 1:
                            for kk in range(2):
                                pr.dma("sync", dbg["d_ss"][kk * 128:(kk + 1) * 128, :],
                                       Ssb[kk][:], s_st, waits=[(s_ev, scmax)])
                        mm = None
                        for kk in range(2):
                            mm = pr.op("tensor", lambda q, kk=kk, base=base: q.matmul(
                                pW[0][0:32, 0:256], WT[kk][:, base:base + 32],
                                Ssb[kk][:], start=(kk == 0), stop=(kk == 1)),
                                waits=[(s_ev, scmax)] if kk == 0 else [], inc=s_mm)
                        eu = pr.op("vector", lambda q, cc=cc: q.tensor_tensor(
                            out=ub[:], in0=Uc[cc][:], in1=pW[0][0:32, 0:256],
                            op=ALU.subtract),
                            waits=[(s_mm, mm), (s_hp, hU[cc])], inc=s_ev)
                        if DBG_D and hl == 0 and mc == 0 and g == 0 and cc == 0:
                            cp = pr.op("vector", lambda q: q.tensor_copy(dbgb[:], ub[:]),
                                       waits=[(s_ev, eu)], inc=s_ev)
                            pr.dma("sync", dbg["d_ub"], dbgb[:], s_st,
                                   waits=[(s_ev, cp)])
                            m9 = pr.op("tensor", lambda q, base=base: q.matmul(
                                pW[0][64:96, 0:256], ATTg[0:32, base:base + 32].rearrange("p c -> p c") if False else ATTg[0:32, base:base + 32],
                                ub[:], start=True, stop=True),
                                waits=[(s_ev, cp)], inc=s_mm)
                            c9 = pr.op("vector", lambda q: q.tensor_copy(
                                dbgb2[:], pW[0][64:96, 0:256]),
                                waits=[(s_mm, m9)], inc=s_ev)
                            pr.dma("sync", dbg["d_att2"], dbgb2[:], s_st,
                                   waits=[(s_ev, c9)])
                        mo = None
                        for kk in range(2):
                            mo = pr.op("tensor", lambda q, kk=kk, g=g, base=base: q.matmul(
                                pW[1][0:32, 0:256],
                                qnb[kk][:, g * 128 + base:g * 128 + base + 32],
                                Ssb[kk][:], start=(kk == 0), stop=False),
                                waits=[(s_ev, scmax)] if kk == 0 else [], inc=s_mm)
                        mo = pr.op("tensor", lambda q, base=base: q.matmul(
                            pW[1][0:32, 0:256], ATTg[0:32, base:base + 32], ub[:],
                            start=False, stop=True),
                            waits=[(s_ev, eu), (s_ev, ea)], inc=s_mm)
                        eo = pr.op("vector", lambda q: q.tensor_copy(
                            ochb[:], pW[1][0:32, 0:256]), waits=[(s_mm, mo)], inc=s_ev)
                        if DBG_D and hl == 0 and mc == 0 and g == 0 and cc == 0:
                            cp2 = pr.op("vector", lambda q: q.tensor_copy(dbgb[:], ochb[:]),
                                        waits=[(s_ev, eo)], inc=s_ev)
                            pr.dma("sync", dbg["d_och"], dbgb[:], s_st,
                                   waits=[(s_ev, cp2)])
                        for hf in range(2):
                            tt = pr.op("tensor", lambda q, hf=hf: q.transpose(
                                pT[:, 0:32], ochb[0:32, hf * 128:(hf + 1) * 128],
                                idb[0:32, 0:32]),
                                waits=[(s_ev, max(eo, s_ev.n))], inc=s_mm)
                            pr.op("vector", lambda q, hf=hf, g=g, base=base: q.tensor_copy(
                                doT[hf][:, g * 128 + base:g * 128 + base + 32],
                                pT[:, 0:32]), waits=[(s_mm, tt)], inc=s_ev)
                        for kk in range(2):
                            w_s = [(s_ev, eu), (s_ev, scmax), (s_hp, hK[cc])]
                            pr.op("tensor", lambda q, kk=kk, cc=cc, first_s=first_s: q.matmul(
                                pS[kk][:, 0:256], knc[cc][:, kk * 128:(kk + 1) * 128],
                                ub[:], start=first_s, stop=False,
                                skip_group_check=True),
                                waits=w_s, inc=s_mm)
                if DBG_D and hl == 0 and mc == 0:
                    for nm, pair in [("d_qn", qnb), ("d_kn", knb), ("d_kb", kbb),
                                     ("d_vb", vbb), ("d_vf", vfb), ("d_fs", fsb),
                                     ("d_fl", flb), ("d_do", doT)]:
                        for hf in range(2):
                            pr.dma("sync", dbg[nm][hf * 128:(hf + 1) * 128, :],
                                   pair[hf][:], s_st, waits=[(s_ev, s_ev.n), (s_ac, s_ac.n)])
                    pr.dma("sync", dbg["d_brow"][:], brow[0:1, :], s_st,
                           waits=[(s_ac, s_ac.n)])
                br_pairs = [fsb, flb, doT, vfb]
                for i, brp in enumerate(br_pairs):
                    for c2 in range(MC // 512):
                        mm = None
                        for hf in range(2):
                            mm = pr.op("tensor", lambda q, hf=hf, c2=c2, brp=brp: q.matmul(
                                pSm[0:1, :], cst["ones128b"][:],
                                brp[hf][:, c2 * 512:(c2 + 1) * 512],
                                start=(hf == 0), stop=(hf == 1)),
                                waits=[(s_ev, s_ev.n), (s_ac, s_ac.n)]
                                if hf == 0 else [], inc=s_mm)
                        pr.op("vector", lambda q, c2=c2: q.tensor_scalar(
                            out=srow[0:1, c2 * 512:(c2 + 1) * 512], in0=pSm[0:1, :],
                            scalar1=1.0 / 256.0, scalar2=None, op0=ALU.mult),
                            waits=[(s_mm, mm), (s_hp, s_hp.n)], inc=s_ev)
                    pr.dma("scalar", strow[3 * i:3 * i + 1, :], srow[0:1, :], s_hp,
                           waits=[(s_ev, s_ev.n)])
                    for c2 in range(MC // 512):
                        mm = None
                        for hf in range(2):
                            sq = pr.op("scalar", lambda q, hf=hf, c2=c2, brp=brp: q.activation(
                                out=sqb[:], in_=brp[hf][:, c2 * 512:(c2 + 1) * 512],
                                func=AF.Square),
                                waits=[(s_mm, s_mm.n), (s_ev, s_ev.n)], inc=s_ac)
                            mm = pr.op("tensor", lambda q, hf=hf: q.matmul(
                                pSm[0:1, :], cst["ones128b"][:], sqb[:],
                                start=(hf == 0), stop=(hf == 1)),
                                waits=[(s_ac, sq)] + ([(s_ev, s_ev.n)] if hf == 0 else []),
                                inc=s_mm)
                        pr.op("vector", lambda q, c2=c2: q.tensor_scalar(
                            out=srow[0:1, c2 * 512:(c2 + 1) * 512], in0=pSm[0:1, :],
                            scalar1=1.0 / 256.0, scalar2=1e-8,
                            op0=ALU.mult, op1=ALU.max),
                            waits=[(s_mm, mm), (s_hp, s_hp.n)], inc=s_ev)
                    sq2 = pr.op("scalar", lambda q: q.activation(
                        out=rrow[0:1, :], in_=srow[0:1, :], func=AF.Sqrt),
                        waits=[(s_ev, s_ev.n), (s_hp, s_hp.n)], inc=s_ac)
                    pr.dma("scalar", strow[3 * i + 1:3 * i + 2, :], rrow[0:1, :], s_hp,
                           waits=[(s_ac, sq2)])
                    for tt in range(8 if MXV else 0):
                        w0 = ([(s_ev, s_ev.n)] if tt == 0
                              else [(s_ev, mxs["ev"][0])])
                        ttr = None
                        for hf in range(2):
                            ttr = pr.op("tensor", lambda q, hf=hf, tt=tt,
                                        brp=brp: q.transpose(
                                pT[:, 256 + hf * 128:256 + (hf + 1) * 128],
                                brp[hf][:, tt * 128:(tt + 1) * 128], cst["idb"][:]),
                                waits=w0 if hf == 0 else [], inc=s_mm)
                        mxs["ev"][0] = pr.op("vector", lambda q, tt=tt:
                                             q.tensor_reduce(
                            mx8col[:, tt:tt + 1], pT[:, 256:512],
                            axis=mybir.AxisListType.X, op=ALU.max),
                            waits=[(s_mm, ttr), (s_mm, mxs["tr"])], inc=s_ev)
                    if MXV == 1:
                        mxs["tr"] = pr.op("tensor", lambda q: q.transpose(
                            pT[0:8, 768:896], mx8col[:], cst["idb"][:]),
                            waits=[(s_ev, mxs["ev"][0]),
                                   (s_ev, mxs["evac"])], inc=s_mm)
                        mxs["evac"] = pr.op("vector", lambda q: q.tensor_copy(
                            mrow8f[:], pT[0:8, 768:896]),
                            waits=[(s_mm, mxs["tr"]), (s_hp, mxs["d2"])], inc=s_ev)
                        d1 = pr.dma("scalar",
                                    dmx_d.rearrange("o (a c) -> (o a) c", a=8),
                                    mrow8f[:], s_hp, waits=[(s_ev, mxs["evac"])])
                        mxs["d2"] = pr.dma("scalar", strow[3 * i + 2:3 * i + 3, :],
                                           dmx_d, s_hp, waits=[(s_hp, d1)])
                    else:
                        tm = pr.op("vector", lambda q, brp=brp: q.tensor_tensor(
                            out=tmp32[0][:], in0=brp[0][:], in1=brp[1][:], op=ALU.max),
                            waits=[(s_ev, s_ev.n)], inc=s_ev)
                        gm = pr.op("gpsimd", lambda q: q.tensor_reduce(
                            srow[0:1, :], tmp32[0][:], axis=mybir.AxisListType.C,
                            op=ALU.max), waits=[(s_ev, tm), (s_hp, s_hp.n)], inc=s_gp)
                        pr.dma("scalar", strow[3 * i + 2:3 * i + 3, :], srow[0:1, :],
                               s_hp, waits=[(s_gp, gm)])
                        if MXV == 2:
                            mxs["tr"] = pr.op("tensor", lambda q: q.transpose(
                                pT[0:8, 768:896], mx8col[:], cst["idb"][:]),
                                waits=[(s_ev, mxs["ev"][0]), (s_ev, mxs["ev"][1]),
                                       (s_ev, mxs["evac"])], inc=s_mm)
                            mxs["evac"] = pr.op("vector", lambda q: q.tensor_copy(
                                mrow8f[:], pT[0:8, 768:896]),
                                waits=[(s_mm, mxs["tr"])], inc=s_ev)
                cst_s = pr.op("vector", lambda q: q.tensor_copy(strowb[:], strow[:]),
                              waits=[(s_hp, s_hp.n)], inc=s_ev)
                brk = [fsb[0], fsb[1], flb[0], flb[1], doT[0], doT[1], vfb[0], vfb[1]]
                for m in range(4):
                    for c2 in range(MC // 512):
                        pi = c2 % 2
                        cs = slice(c2 * 512, (c2 + 1) * 512)
                        w0 = [(s_ev, cst_s), (s_ac, s_ac.n)]
                        pr.op("tensor", lambda q, m=m, cs=cs, pi=pi: q.matmul(
                            pQ[pi][:], g1hv[:, 0, m * 128:(m + 1) * 128],
                            xqv[:, 0, cs], start=True, stop=False),
                            waits=w0)
                        for k in range(1, 8):
                            pr.op("tensor", lambda q, k=k, m=m, cs=cs, pi=pi: q.matmul(
                                pQ[pi][:], g1hv[:, k, m * 128:(m + 1) * 128],
                                xqv[:, k, cs], start=False, stop=False))
                        pr.op("tensor", lambda q, m=m, cs=cs, pi=pi: q.matmul(
                            pQ[pi][:], statwb[:, m * 128:(m + 1) * 128],
                            strowb[:, cs], start=False, stop=False))
                        for k in range(8):
                            pr.op("tensor", lambda q, k=k, m=m, cs=cs, pi=pi: q.matmul(
                                pQ[pi][:], g1bv[:, k, m * 128:(m + 1) * 128],
                                brk[k][:, cs], start=False, stop=(k == 7)),
                                inc=s_mm if k == 7 else None)
                        pr.op("scalar", lambda q, m=m, c2=c2, pi=pi: q.activation(
                            out=hmid[:, m * MC + c2 * 512:m * MC + (c2 + 1) * 512],
                            in_=pQ[pi][:], func=AF.Gelu,
                            bias=cst["gb1v"][:, m:m + 1]),
                            waits=[(s_mm, s_mm.n)], inc=s_ac)
                for tt8 in range(8):
                    mm = None
                    for kt in range(4):
                        mm = pr.op("tensor", lambda q, kt=kt, tt8=tt8: q.matmul(
                            pSm[0:128, 0:4],
                            hmid[:, kt * MC + tt8 * 128:kt * MC + (tt8 + 1) * 128],
                            g2v[:, kt, :], start=(kt == 0), stop=(kt == 3)),
                            waits=[(s_ac, s_ac.n), (s_ev, s_ev.n)]
                            if kt == 0 else [], inc=s_mm if kt == 3 else None)
                    e1 = pr.op("vector", lambda q: q.tensor_tensor(
                        out=lgt[:, 0:4], in0=pSm[0:128, 0:4], in1=cst["gb2b"][:],
                        op=ALU.add), waits=[(s_mm, mm)], inc=s_ev)
                    e2 = pr.op("vector", lambda q: q.tensor_reduce(
                        lgt[:, 4:5], lgt[:, 0:4], axis=mybir.AxisListType.X,
                        op=ALU.max), waits=[(s_ev, e1)], inc=s_ev)
                    e3 = pr.op("vector", lambda q, hl=hl: q.tensor_scalar(
                        out=lgt[:, 5:6], in0=lgt[:, 4:5],
                        scalar1=invt[:, hl:hl + 1], scalar2=-1.0,
                        op0=ALU.mult, op1=ALU.mult), waits=[(s_ev, e2)], inc=s_ev)
                    e4 = pr.op("scalar", lambda q, hl=hl: q.activation(
                        out=wfin[:], in_=lgt[:, 0:4], func=AF.Exp,
                        scale=invt[:, hl:hl + 1], bias=lgt[:, 5:6]),
                        waits=[(s_ev, e3)], inc=s_ac)
                    e5 = pr.op("vector", lambda q: q.tensor_reduce(
                        lgt[:, 6:7], wfin[:], axis=mybir.AxisListType.X,
                        op=ALU.add), waits=[(s_ac, e4)], inc=s_ev)
                    e6 = pr.op("vector", lambda q: q.reciprocal(
                        lgt[:, 7:8], lgt[:, 6:7]), waits=[(s_ev, e5)], inc=s_ev)
                    e7 = pr.op("vector", lambda q: q.tensor_scalar(
                        out=lgt[:, 0:4], in0=wfin[:], scalar1=lgt[:, 7:8],
                        scalar2=None, op0=ALU.mult), waits=[(s_ev, e6)], inc=s_ev)
                    e8 = pr.op("vector", lambda q, hl=hl: q.tensor_tensor(
                        out=wfin[:], in0=lgt[:, 0:4],
                        in1=flc[:, hl * 4:(hl + 1) * 4], op=ALU.max),
                        waits=[(s_ev, e7)], inc=s_ev)
                    e9 = pr.op("vector", lambda q: q.tensor_reduce(
                        lgt[:, 6:7], wfin[:], axis=mybir.AxisListType.X,
                        op=ALU.add), waits=[(s_ev, e8)], inc=s_ev)
                    ea_ = pr.op("vector", lambda q: q.reciprocal(
                        lgt[:, 7:8], lgt[:, 6:7]), waits=[(s_ev, e9)], inc=s_ev)
                    pr.op("vector", lambda q, tt8=tt8: q.tensor_scalar(
                        out=wfin8[:, tt8 * 4:(tt8 + 1) * 4], in0=wfin[:],
                        scalar1=lgt[:, 7:8], scalar2=None, op0=ALU.mult),
                        waits=[(s_ev, ea_)], inc=s_ev)
                for i, brp in enumerate(br_pairs):
                    for tt8 in range(8):
                        tt = pr.op("tensor", lambda q, i=i, tt8=tt8: q.transpose(
                            pW[0][0:1, 0:128], wfin8[:, tt8 * 4 + i:tt8 * 4 + i + 1],
                            cst["idf"][:]),
                            waits=[(s_ev, s_ev.n)], inc=s_mm)
                        pr.op("vector", lambda q, tt8=tt8: q.tensor_copy(
                            brow[0:1, tt8 * 128:(tt8 + 1) * 128], pW[0][0:1, 0:128]),
                            waits=[(s_mm, tt)], inc=s_ev)
                    bcast_row(brow, bcast)
                    for hf in range(2):
                        if i == 0:
                            pr.op("vector", lambda q, hf=hf, brp=brp: q.tensor_tensor(
                                out=o32[hf][:], in0=brp[hf][:], in1=bcast[:],
                                op=ALU.mult), waits=[(s_ev, s_ev.n)], inc=s_ev)
                        else:
                            t1 = pr.op("vector", lambda q, hf=hf, brp=brp: q.tensor_tensor(
                                out=tmp32[1][:], in0=brp[hf][:], in1=bcast[:],
                                op=ALU.mult), waits=[(s_ev, s_ev.n)], inc=s_ev)
                            pr.op("vector", lambda q, hf=hf: q.tensor_tensor(
                                out=o32[hf][:], in0=o32[hf][:], in1=tmp32[1][:],
                                op=ALU.add), waits=[(s_ev, t1)], inc=s_ev)
                if DBG_D and hl == 0 and mc == 0:
                    pr.dma("sync", dbg["d_st"][:], strow[:], s_st,
                           waits=[(s_ev, s_ev.n)])
                    pr.dma("sync", dbg["d_w8"][:], wfin8[:], s_st,
                           waits=[(s_ev, s_ev.n)])
                    pr.dma("sync", dbg["d_hm"][:], hmid[:], s_st,
                           waits=[(s_ev, s_ev.n), (s_ac, s_ac.n)])
                    pr.dma("sync", dbg["d_o0"][:], o32[0][:], s_st,
                           waits=[(s_ev, s_ev.n)])
                for c2 in range(MC // 512):
                    mm = None
                    for hf in range(2):
                        sq = pr.op("scalar", lambda q, hf=hf, c2=c2: q.activation(
                            out=sqb[:], in_=o32[hf][:, c2 * 512:(c2 + 1) * 512],
                            func=AF.Square),
                            waits=[(s_mm, s_mm.n), (s_ev, s_ev.n)], inc=s_ac)
                        mm = pr.op("tensor", lambda q, hf=hf: q.matmul(
                            pSm[0:1, :], cst["ones128b"][:], sqb[:],
                            start=(hf == 0), stop=(hf == 1)),
                            waits=[(s_ac, sq)] + ([(s_ev, s_ev.n)] if hf == 0 else []),
                            inc=s_mm)
                    pr.op("vector", lambda q, c2=c2: q.tensor_scalar(
                        out=srow[0:1, c2 * 512:(c2 + 1) * 512], in0=pSm[0:1, :],
                        scalar1=1.0 / 256.0, scalar2=1e-5,
                        op0=ALU.mult, op1=ALU.add), waits=[(s_mm, mm)], inc=s_ev)
                r1 = pr.op("scalar", lambda q: q.activation(
                    out=rrow[0:1, :], in_=srow[0:1, :], func=AF.Ln),
                    waits=[(s_ev, s_ev.n)], inc=s_ac)
                r2 = pr.op("scalar", lambda q: q.activation(
                    out=srow[0:1, :], in_=rrow[0:1, :], func=AF.Exp, scale=-0.5),
                    waits=[(s_ac, r1)], inc=s_ac)
                bcast_row(srow, bcast, extra_waits=[(s_ac, r2)])
                for hf in range(2):
                    eo = pr.op("vector", lambda q, hf=hf: q.scalar_tensor_tensor(
                        out=obf[:], in0=o32[hf][:],
                        scalar=cst["onormv"][:, hf:hf + 1], in1=bcast[:],
                        op0=ALU.mult, op1=ALU.mult),
                        waits=[(s_ev, s_ev.n), (s_st, s_st.n)], inc=s_ev)
                    pr.dma("sync", oT_d[hl * 256 + hf * 128:hl * 256 + (hf + 1) * 128,
                                        mc * MC:(mc + 1) * MC],
                           obf[:], s_st, waits=[(s_ev, eo)])
        pr.wait("gpsimd", [(s_st, s_st.n)])
        pr.finish()
    return nc


def host_prep_D(inputs, resC):
    maps = []
    idb = np.eye(128, dtype=bf16)
    idf = np.eye(128, dtype=np.float32)
    mneg = np.zeros((128, 128), np.float32)
    for cb in range(4):
        for s_ in range(32):
            for t_ in range(32):
                if s_ < t_:
                    mneg[cb * 32 + s_, cb * 32 + t_] = -1.0
    miu = np.zeros((32, 128), np.float32)
    for cb in range(4):
        for s_ in range(32):
            for t_ in range(32):
                if s_ <= t_:
                    miu[s_, cb * 32 + t_] = 1.0
    gw1 = inputs["L1_attn_gW1"]
    for c in range(NC):
        b, hp = c // 2, c % 2
        sl = slice(hp * 512, (hp + 1) * 512)
        xb = np.concatenate([resC[2 * b]["xn1t"], resC[2 * b + 1]["xn1t"]], axis=1)
        fir_s = inputs["L1_attn_fir_s"].reshape(1024, 3)[sl].reshape(4, 128, 3)
        fir_l = inputs["L1_attn_fir_l"].reshape(1024, 31)[sl].reshape(4, 128, 31)
        m = {
            "xn1t_b": np.ascontiguousarray(xb),
            "wq": np.ascontiguousarray(inputs["L1_attn_Wq"][:, sl]),
            "wk": np.ascontiguousarray(inputs["L1_attn_Wk"][:, sl]),
            "wv": np.ascontiguousarray(inputs["L1_attn_Wv"][:, sl]),
            "wb2": np.ascontiguousarray(inputs["L1_attn_Wb"][:, 2 * hp:2 * hp + 2]),
            "qcw": np.ascontiguousarray(
                inputs["L1_attn_qc_w"][sl].reshape(4, 128, 4).transpose(1, 0, 2).reshape(128, 16)),
            "kcw": np.ascontiguousarray(
                inputs["L1_attn_kc_w"][sl].reshape(4, 128, 4).transpose(1, 0, 2).reshape(128, 16)),
            "vcw": np.ascontiguousarray(
                inputs["L1_attn_vc_w"][sl].reshape(4, 128, 4).transpose(1, 0, 2).reshape(128, 16)),
            "firs": np.ascontiguousarray(fir_s.transpose(1, 0, 2).reshape(128, 12)),
            "firl": np.ascontiguousarray(fir_l.transpose(1, 0, 2).reshape(128, 124)),
            "gw1h": np.ascontiguousarray(gw1[0:1024]),
            "gw1s": np.ascontiguousarray(gw1[1024:4096]),
            "gw1b": np.ascontiguousarray(gw1[4096:5120]),
            "gw2": np.ascontiguousarray(inputs["L1_attn_gW2"]),
            "gb1v": np.ascontiguousarray(inputs["L1_attn_gb1"].reshape(4, 128).T),
            "gb2b": np.ascontiguousarray(np.broadcast_to(
                inputs["L1_attn_gb2"][None, :], (128, 4))),
            "tmpb": np.ascontiguousarray(np.broadcast_to(
                inputs["L1_attn_temp"][None, 2 * hp:2 * hp + 2], (128, 2))),
            "flrb": np.ascontiguousarray(np.broadcast_to(
                inputs["L1_attn_eps_floor"][2 * hp:2 * hp + 2].reshape(1, 8), (128, 8))),
            "onormv": np.ascontiguousarray(
                inputs["L1_attn_onorm_w"].reshape(2, 128).T),
            "mneg": mneg, "miu": miu, "idb": idb, "idf": idf,
            "ones1f": np.ones((1, 128), np.float32),
            "ones128b": np.ones((128, 1), dtype=bf16),
            "ones128f": np.ones((128, 1), np.float32),
            "kn": KONST,
        }
        maps.append(m)
    return maps


DBG_E = False


def build_E():
    nc = bass.Bass("TRN2", target_bir_lowering=False, debug=False)
    dr = {}
    for nm, shp, dt in [
        ("x1", (TSH, D), F32), ("xn1t", (D, TSH), BF16),
        ("o_sl", (D, TSH), BF16), ("wo1", (D, D), F32),
        ("wgt", (D, NEXP), F32), ("bgb", (128, NEXP), F32),
        ("wgu_p", (NEXP, D, 2 * EF), F32), ("bgu_p", (128, 256), F32),
        ("wd", (NEXP, EF, D), F32), ("bd", (NEXP, D), F32),
        ("iota", (128, NEXP), F32), ("idf", (128, 128), F32),
        ("ones1f", (1, 128), F32), ("ones1b", (1, 128), BF16),
        ("kn", (128, 8), F32),
    ]:
        dr[nm] = nc.dram_tensor(nm, list(shp), dt, kind="ExternalInput").ap()
    out_d = nc.dram_tensor("out", [TSH, D], F32, kind="ExternalOutput").ap()
    T2_ = TSH // 2
    dbg = {}
    if DBG_E:
        for nm, shp, dt in [("e_acc", (128, 8 * T2_), F32),
                            ("e_cw", (16, T2_), BF16),
                            ("e_gat", (128, 8 * T2_), BF16),
                            ("e_acc0", (128, 8 * T2_), F32)]:
            dbg[nm] = nc.dram_tensor(nm, list(shp), dt, kind="ExternalOutput").ap()

    T2 = TSH // 2
    NCK = T2 // 512

    with ExitStack() as ctx:
        pr = Prog(nc, ctx)
        s0 = pr.sem("e0")
        s_ld = pr.sem("eld")
        s_mm = pr.sem("emm")
        s_ev = pr.sem("eev")
        s_ac = pr.sem("eac")
        s_gp = pr.sem("egp")
        s_st = pr.sem("est")
        s_hp = pr.sem("ehp")

        iota = sb(ctx, nc, "iota_s", (128, NEXP), F32)
        ones1f = sb(ctx, nc, "o1f_s", (1, 128), F32)
        ones1b = sb(ctx, nc, "o1b_s", (1, 128), BF16)
        idf = sb(ctx, nc, "idf_s", (128, 128), F32)
        bgb = sb(ctx, nc, "bgb_s", (128, NEXP), F32)
        bgu = sb(ctx, nc, "bgu_s", (128, 256), F32)
        wgtb = sb(ctx, nc, "wgt_b", (128, 8 * NEXP), BF16)
        bdb = sb(ctx, nc, "bd_b", (16, D), BF16)
        xnth = sb(ctx, nc, "xnth", (128, 8 * T2), BF16)
        osb = xnth
        acc = sb(ctx, nc, "acc", (128, 8 * T2), F32)
        gated = sb(ctx, nc, "gated", (128, 8 * T2), BF16)
        wgb = sb(ctx, nc, "wgb", (128, 8 * EF), BF16)
        wub = sb(ctx, nc, "wub", (128, 8 * EF), BF16)
        wdb = sb(ctx, nc, "wdb", (128, 8 * D), BF16)
        lgt16 = sb(ctx, nc, "lgt16", (128, NEXP), F32)
        mx8 = sb(ctx, nc, "mx8", (128, 8), F32)
        ix8 = sb(ctx, nc, "ix8", (128, 8), U32)
        ixf = sb(ctx, nc, "ixf", (128, 2), F32)
        wcol = sb(ctx, nc, "wcol", (128, 3), F32)
        msk = sb(ctx, nc, "msk", (128, NEXP), F32)
        cwt = sb(ctx, nc, "cwt", (128, NEXP), F32)
        cwTb = sb(ctx, nc, "cwTb", (16, T2), BF16)
        cwrow = sb(ctx, nc, "cwrow", (1, T2), BF16)
        cwbB = sb(ctx, nc, "cwbB", (128, T2), BF16)
        sil = sb(ctx, nc, "sil", (128, 512), F32)
        sil2 = sb(ctx, nc, "sil2", (128, 512), F32)
        upt = sb(ctx, nc, "upt", (128, 512), F32)
        x1t = sb(ctx, nc, "x1t", (128, D), F32)
        silk = sb(ctx, nc, "silk", (128, 128), F32)
        sgb = sb(ctx, nc, "sgb", (128, 128), F32)
        pA = psum_bank(ctx, nc, "epA")
        pB = psum_bank(ctx, nc, "epB")
        pC = [psum_bank(ctx, nc, f"epC{i}") for i in range(2)]
        pT2 = psum_bank(ctx, nc, "epT2")
        pApp = psum_bank(ctx, nc, "epApp")
        pBpp = psum_bank(ctx, nc, "epBpp")

        xnv = xnth.rearrange("p (a c) -> p a c", a=8)
        ov = osb.rearrange("p (a c) -> p a c", a=8)
        wgv = wgtb.rearrange("p (a c) -> p a c", a=8)
        wgbv = wgb.rearrange("p (a c) -> p a c", a=8)
        wubv = wub.rearrange("p (a c) -> p a c", a=8)
        wdv = wdb.rearrange("p (a c) -> p a c", a=8)

        accv = acc.rearrange("p (a c) -> p a c", a=8)
        gav = gated.rearrange("p (a c) -> p a c", a=8)

        for nm, t in [("iota", iota), ("idf", idf), ("bgb", bgb), ("bgu_p", bgu),
                      ("ones1f", ones1f), ("ones1b", ones1b)]:
            pr.dma("sync", t[:], dr[nm], s0)
        pr.dma("gpsimd", kt_dst(wgtb[:], 8), rearr_kt(dr["wgt"]), s_gp)
        pr.dma("gpsimd", bdb[:], dr["bd"], s_gp)
        gp0 = s_gp.n
        for e in ("tensor", "scalar", "vector", "gpsimd"):
            pr.wait(e, [(s0, s0.n), (s_gp, gp0)])

        st_all = 0
        st_ev = {"u3": 0, "mmb": 0}
        for th in range(2):
            cols = slice(th * T2, (th + 1) * T2)
            lx = pr.dma("sync", kt_dst(xnth[:], 8),
                        rearr_kt(dr["xn1t"][:, cols]), s_ld,
                        waits=[(s_mm, s_mm.n)])
            for tt8 in range(T2 // 128):
                mm = None
                for k in range(8):
                    w = [(s_ld, lx), (s_ev, s_ev.n)] if k == 0 else []
                    mm = pr.op("tensor", lambda q, k=k, tt8=tt8: q.matmul(
                        pA[0:128, 0:NEXP], xnv[:, k, tt8 * 128:(tt8 + 1) * 128],
                        wgv[:, k, :], start=(k == 0), stop=(k == 7)),
                        waits=w, inc=s_mm if k == 7 else None)
                e1 = pr.op("vector", lambda q: q.tensor_tensor(
                    out=lgt16[:], in0=pA[0:128, 0:NEXP], in1=bgb[:], op=ALU.add),
                    waits=[(s_mm, mm)], inc=s_ev)
                e2 = pr.op("vector", lambda q: q.max(mx8[:], lgt16[:]),
                           waits=[(s_ev, e1)], inc=s_ev)
                e3 = pr.op("vector", lambda q: q.max_index(ix8[:], mx8[:], lgt16[:]),
                           waits=[(s_ev, e2)], inc=s_ev)
                e4 = pr.op("vector", lambda q: q.tensor_copy(ixf[:], ix8[:, 0:2]),
                           waits=[(s_ev, e3)], inc=s_ev)
                e5 = pr.op("vector", lambda q: q.tensor_tensor(
                    out=wcol[:, 0:1], in0=mx8[:, 0:1], in1=mx8[:, 1:2],
                    op=ALU.subtract), waits=[(s_ev, e3)], inc=s_ev)
                a1 = pr.op("scalar", lambda q: q.activation(
                    out=wcol[:, 1:2], in_=wcol[:, 0:1], func=AF.Sigmoid),
                    waits=[(s_ev, e5)], inc=s_ac)
                a2 = pr.op("scalar", lambda q: q.activation(
                    out=wcol[:, 2:3], in_=wcol[:, 0:1], func=AF.Sigmoid,
                    scale=-1.0), waits=[(s_ev, e5), (s_ac, a1)], inc=s_ac)
                e6 = pr.op("vector", lambda q: q.tensor_scalar(
                    out=msk[:], in0=iota[:], scalar1=ixf[:, 0:1], scalar2=None,
                    op0=ALU.is_equal), waits=[(s_ev, e4), (s_ac, a2)], inc=s_ev)
                e7 = pr.op("vector", lambda q: q.tensor_scalar(
                    out=cwt[:], in0=msk[:], scalar1=wcol[:, 1:2], scalar2=None,
                    op0=ALU.mult), waits=[(s_ev, e6)], inc=s_ev)
                e8 = pr.op("vector", lambda q: q.tensor_scalar(
                    out=msk[:], in0=iota[:], scalar1=ixf[:, 1:2], scalar2=None,
                    op0=ALU.is_equal), waits=[(s_ev, e7)], inc=s_ev)
                e9 = pr.op("vector", lambda q: q.scalar_tensor_tensor(
                    out=cwt[:], in0=msk[:], scalar=wcol[:, 2:3], in1=cwt[:],
                    op0=ALU.mult, op1=ALU.add), waits=[(s_ev, e8)], inc=s_ev)
                tt = pr.op("tensor", lambda q: q.transpose(
                    pT2[0:NEXP, 0:128], cwt[:], idf[:]),
                    waits=[(s_ev, e9)], inc=s_mm)
                pr.op("vector", lambda q, tt8=tt8: q.tensor_copy(
                    cwTb[:, tt8 * 128:(tt8 + 1) * 128], pT2[0:NEXP, 0:128]),
                    waits=[(s_mm, tt)], inc=s_ev)
            cwc = s_ev.n
            for m in range(8):
                for c in range(NCK):
                    pi = c % 2
                    mm = pr.op("tensor", lambda q, m=m, c=c, pi=pi: q.matmul(
                        pC[pi][:], bdb[:, m * 128:(m + 1) * 128],
                        cwTb[:, c * 512:(c + 1) * 512], start=True, stop=True),
                        waits=[(s_ev, s_ev.n)], inc=s_mm)
                    pr.op("vector", lambda q, m=m, c=c, pi=pi: q.tensor_copy(
                        accv[:, m, c * 512:(c + 1) * 512], pC[pi][:]),
                        waits=[(s_mm, mm)], inc=s_ev)
            if DBG_E and th == 0:
                pr.dma("sync", dbg["e_acc0"], acc[:], s_st, waits=[(s_ev, s_ev.n)])
            pAg = [pA, pApp]
            pBg = [pB, pBpp]
            lg = pr.dma("gpsimd", kt_dst(wgb[:], 8),
                        rearr_kt(dr["wgu_p"][0][:, 0:EF]), s_gp,
                        waits=[(s_mm, s_mm.n)])
            lu = pr.dma("gpsimd", kt_dst(wub[:], 8),
                        rearr_kt(dr["wgu_p"][0][:, EF:2 * EF]), s_gp,
                        waits=[(s_mm, s_mm.n)])
            ld_ = pr.dma("gpsimd", kt_dst(wdb[:], 8),
                         rearr_kt(dr["wd"][0]), s_gp,
                         waits=[(s_mm, s_mm.n)])
            evA = [0, 0]
            evB = [0, 0]
            evC = [0, 0]
            g2p = 0
            for e in range(NEXP):
                hp_ = pr.dma("scalar", cwrow[:], cwTb[e:e + 1, :], s_hp,
                             waits=[(s_ev, cwc), (s_mm, st_ev["mmb"])])
                bl = 0
                for c in range(NCK):
                    st_ev["mmb"] = pr.op("tensor", lambda q, c=c: q.matmul(
                        pT2[:, 0:512], ones1b[:],
                        cwrow[0:1, c * 512:(c + 1) * 512], start=True, stop=True),
                        waits=([(s_hp, hp_), (s_ev, st_ev["u3"])] if c == 0
                               else [(s_ev, bl)]), inc=s_mm)
                    bl = pr.op("vector", lambda q, c=c: q.tensor_copy(
                        cwbB[:, c * 512:(c + 1) * 512], pT2[:, 0:512]),
                        waits=[(s_mm, st_ev["mmb"])], inc=s_ev)
                gmm_last = 0
                umm_last = 0
                for m in range(8):
                    for c in range(NCK):
                        pi = (m * NCK + c) % 2
                        mm = None
                        for k in range(8):
                            w = [(s_gp, lg), (s_ev, evA[pi])] if k == 0 else []
                            mm = pr.op("tensor", lambda q, k=k, m=m, c=c, pi=pi: q.matmul(
                                pAg[pi][:], wgbv[:, k, m * 128:(m + 1) * 128],
                                xnv[:, k, c * 512:(c + 1) * 512],
                                start=(k == 0), stop=(k == 7)),
                                waits=w, inc=s_mm if k == 7 else None)
                        gmm_last = mm
                        g1 = pr.op("vector", lambda q, m=m, e=e, pi=pi: q.tensor_scalar(
                            out=sil[:], in0=pAg[pi][:], scalar1=bgu[:, e * 8 + m:e * 8 + m + 1],
                            scalar2=7.0, op0=ALU.add, op1=ALU.min),
                            waits=[(s_mm, mm), (s_ac, g2p)], inc=s_ev)
                        evA[pi] = g1
                        g2p = pr.op("scalar", lambda q: q.activation(
                            out=sil2[:], in_=sil[:], func=AF.Silu, scale=1.702),
                            waits=[(s_ev, g1)], inc=s_ac)
                        mm = None
                        for k in range(8):
                            w = [(s_gp, lu), (s_ev, evB[pi])] if k == 0 else []
                            mm = pr.op("tensor", lambda q, k=k, m=m, c=c, pi=pi: q.matmul(
                                pBg[pi][:], wubv[:, k, m * 128:(m + 1) * 128],
                                xnv[:, k, c * 512:(c + 1) * 512],
                                start=(k == 0), stop=(k == 7)),
                                waits=w, inc=s_mm if k == 7 else None)
                        umm_last = mm
                        u1 = pr.op("vector", lambda q, m=m, e=e, pi=pi: q.tensor_scalar(
                            out=upt[:], in0=pBg[pi][:],
                            scalar1=bgu[:, 128 + e * 8 + m:128 + e * 8 + m + 1],
                            scalar2=7.0, op0=ALU.add, op1=ALU.min),
                            waits=[(s_mm, mm)], inc=s_ev)
                        evB[pi] = u1
                        u2 = pr.op("vector", lambda q: q.tensor_scalar(
                            out=upt[:], in0=upt[:], scalar1=-7.0, scalar2=1.0,
                            op0=ALU.max, op1=ALU.add),
                            waits=[(s_ev, u1)], inc=s_ev)
                        u3 = pr.op("vector", lambda q, c=c: q.scalar_tensor_tensor(
                            out=upt[:], in0=upt[:], scalar=1.0 / 1.702,
                            in1=cwbB[:, c * 512:(c + 1) * 512],
                            op0=ALU.mult, op1=ALU.mult),
                            waits=[(s_ev, u2), (s_ev, bl)], inc=s_ev)
                        st_ev["u3"] = u3
                        pr.op("vector", lambda q, m=m, c=c: q.tensor_tensor(
                            out=gav[:, m, c * 512:(c + 1) * 512], in0=sil2[:],
                            in1=upt[:], op=ALU.mult),
                            waits=[(s_ev, u3), (s_ac, g2p)], inc=s_ev)
                gated_done = s_ev.n
                if e + 1 < NEXP:
                    lg = pr.dma("gpsimd", kt_dst(wgb[:], 8),
                                rearr_kt(dr["wgu_p"][e + 1][:, 0:EF]), s_gp,
                                waits=[(s_mm, gmm_last)])
                    lu = pr.dma("gpsimd", kt_dst(wub[:], 8),
                                rearr_kt(dr["wgu_p"][e + 1][:, EF:2 * EF]), s_gp,
                                waits=[(s_mm, umm_last)])
                dmm_last = 0
                for m in range(8):
                    for c in range(NCK):
                        pi = c % 2
                        mm = None
                        for k in range(8):
                            w = ([(s_gp, ld_), (s_ev, gated_done),
                                  (s_ev, evC[pi])] if k == 0 else [])
                            mm = pr.op("tensor", lambda q, k=k, m=m, c=c, pi=pi: q.matmul(
                                pC[pi][:], wdv[:, k, m * 128:(m + 1) * 128],
                                gav[:, k, c * 512:(c + 1) * 512],
                                start=(k == 0), stop=(k == 7)),
                                waits=w, inc=s_mm if k == 7 else None)
                        dmm_last = mm
                        evC[pi] = pr.op("vector", lambda q, m=m, c=c, pi=pi: q.tensor_tensor(
                            out=accv[:, m, c * 512:(c + 1) * 512], in0=pC[pi][:],
                            in1=accv[:, m, c * 512:(c + 1) * 512], op=ALU.add),
                            waits=[(s_mm, mm)], inc=s_ev)
                if e + 1 < NEXP:
                    ld_ = pr.dma("gpsimd", kt_dst(wdb[:], 8),
                                 rearr_kt(dr["wd"][e + 1]), s_gp,
                                 waits=[(s_mm, dmm_last)])
            if DBG_E and th == 0:
                pr.dma("sync", dbg["e_acc"], acc[:], s_st,
                       waits=[(s_ev, s_ev.n)])
                pr.dma("sync", dbg["e_gat"], gated[:], s_st,
                       waits=[(s_ev, s_ev.n)])
                pr.dma("sync", dbg["e_cw"], cwTb[:], s_st,
                       waits=[(s_ev, s_ev.n)])
            lo = pr.dma("sync", kt_dst(osb[:], 8), rearr_kt(dr["o_sl"][:, cols]),
                        s_ld, waits=[(s_mm, s_mm.n)])
            lwo = pr.dma("gpsimd", kt_dst(wdb[:], 8), rearr_kt(dr["wo1"]), s_gp,
                         waits=[(s_mm, s_mm.n)])
            for i in range(T2 // 128):
                gi = th * T2 + i * 128
                lx1 = pr.dma("sync", x1t[:], dr["x1"][gi:gi + 128, :], s_ld,
                             waits=[(s_ev, s_ev.n), (s_st, s_st.n)])
                for m in range(8):
                    mm = None
                    for k in range(8):
                        w = [(s_ld, lo), (s_gp, lwo), (s_ev, s_ev.n)] if k == 0 else []
                        mm = pr.op("tensor", lambda q, k=k, m=m, i=i: q.matmul(
                            pA[:, 0:128], wdv[:, k, m * 128:(m + 1) * 128],
                            ov[:, k, i * 128:(i + 1) * 128],
                            start=(k == 0), stop=(k == 7)),
                            waits=w, inc=s_mm if k == 7 else None)
                    sa = pr.op("scalar", lambda q, m=m, i=i: q.activation(
                        out=silk[:], in_=accv[:, m, i * 128:(i + 1) * 128],
                        func=AF.Silu), waits=[(s_ev, s_ev.n), (s_ac, s_ac.n)],
                        inc=s_ac)
                    sg = pr.op("vector", lambda q: q.tensor_tensor(
                        out=sgb[:], in0=pA[:, 0:128], in1=silk[:], op=ALU.mult),
                        waits=[(s_mm, mm), (s_ac, sa)], inc=s_ev)
                    tt = pr.op("tensor", lambda q: q.transpose(
                        pT2[:, 0:128], sgb[:], idf[:]),
                        waits=[(s_ev, sg)], inc=s_mm)
                    pr.op("vector", lambda q, m=m: q.tensor_tensor(
                        out=x1t[:, m * 128:(m + 1) * 128], in0=pT2[:, 0:128],
                        in1=x1t[:, m * 128:(m + 1) * 128], op=ALU.add),
                        waits=[(s_mm, tt), (s_ld, lx1)], inc=s_ev)
                st_all = pr.dma("sync", out_d[gi:gi + 128, :], x1t[:], s_st,
                                waits=[(s_ev, s_ev.n)])
        pr.wait("gpsimd", [(s_st, s_st.n)])
        pr.finish()
    return nc


def host_prep_E(inputs, resC, resD):
    idf = np.eye(128, dtype=np.float32)
    iota = np.ascontiguousarray(np.broadcast_to(
        np.arange(NEXP, dtype=np.float32)[None, :], (128, NEXP)))
    wgu = inputs["L1_ffn_Wgu"]
    wgu_p = np.ascontiguousarray(np.concatenate(
        [wgu[:, :, 0::2], wgu[:, :, 1::2]], axis=2))
    bgu = inputs["L1_ffn_bgu"]
    bgu_p = np.concatenate([bgu[:, 0::2], bgu[:, 1::2]], axis=1)
    bgu_t = np.zeros((128, 256), np.float32)
    for e in range(NEXP):
        bgu_t[:, e * 8:e * 8 + 8] = bgu_p[e][0:1024].reshape(8, 128).T
        bgu_t[:, 128 + e * 8:128 + e * 8 + 8] = bgu_p[e][1024:2048].reshape(8, 128).T
    maps = []
    for c in range(NC):
        b, half = c // 2, c % 2
        ob = np.concatenate([resD[2 * b]["oT"], resD[2 * b + 1]["oT"]], axis=0)
        maps.append({
            "x1": resC[c]["x1"],
            "xn1t": resC[c]["xn1t"],
            "o_sl": np.ascontiguousarray(ob[:, half * TSH:(half + 1) * TSH]),
            "wo1": inputs["L1_attn_Wo"],
            "wgt": np.ascontiguousarray(inputs["L1_ffn_Wg"].T),
            "bgb": np.ascontiguousarray(np.broadcast_to(
                inputs["L1_ffn_bg"][None, :], (128, NEXP))),
            "wgu_p": wgu_p,
            "bgu_p": bgu_t,
            "wd": inputs["L1_ffn_Wd"],
            "bd": inputs["L1_ffn_bd"],
            "iota": iota, "idf": idf, "ones1f": np.ones((1, 128), np.float32),
            "ones1b": np.ones((1, 128), dtype=bf16), "kn": KONST,
        })
    return maps


def kernel(**inputs):
    inputs = {k: np.asarray(v) for k, v in inputs.items()}
    cores = list(range(NC))
    resA = run_bass_kernel_spmd(build_A(), host_prep_A(inputs), core_ids=cores).results
    xnt_full = np.concatenate([resA[c]["xnt"] for c in range(NC)], axis=0)
    resB = run_bass_kernel_spmd(build_B(), host_prep_B(inputs, xnt_full), core_ids=cores).results
    resC = run_bass_kernel_spmd(build_C(), host_prep_C(inputs, resA, resB), core_ids=cores).results
    resD = run_bass_kernel_spmd(build_D(), host_prep_D(inputs, resC), core_ids=cores).results
    resE = run_bass_kernel_spmd(build_E(), host_prep_E(inputs, resC, resD), core_ids=cores).results
    out = np.concatenate([np.asarray(resE[c]["out"]) for c in range(NC)], axis=0)
    return out.reshape(B, S, D)
